# version 16
# baseline (speedup 1.0000x reference)
import numpy as np
import concourse.bass as bass
import concourse.mybir as mybir
from concourse.bass_utils import run_bass_kernel_spmd

F32 = mybir.dt.float32
BF16 = mybir.dt.bfloat16
U8 = mybir.dt.uint8
AF = mybir.ActivationFunctionType
ALU = mybir.AluOpType
AX = mybir.AxisListType

D = 1024
DI = 2048
NST = 16
DTR = 128
NH = 16
DH = 64
NP = NH * (DH + 1)
NCORES = 8
ARENA_BYTES = 147296
SBASE = 4416

ENGINES = ("sp", "gp", "pe", "act", "ve")
SEMS = ("d_sp", "d_sp2", "d_spb", "d_gp", "d_gp2", "d_gpb", "c_pe", "c_act",
        "c_ve", "c_gp", "c_sp", "b_sp", "b_gp", "b_pe", "b_act", "b_ve")
BAR = {"sp": "b_sp", "gp": "b_gp", "pe": "b_pe", "act": "b_act", "ve": "b_ve"}


class Rec:

    def __init__(self):
        self.ops = {e: [] for e in ENGINES}
        self.cnt = {s: 0 for s in SEMS}
        self.sems = None

    def _emit(self, eng, fn):
        self.ops[eng].append(fn)

    def op(self, eng, build, inc=None, inck=1):
        if inc is not None:
            self.cnt[inc] += inck
        def f(e, build=build, inc=inc, inck=inck):
            ins = build(e)
            if inc is not None:
                ins.then_inc(self.sems[inc], inck)
        self._emit(eng, f)
        return dict(self.cnt)

    def wait(self, eng, sem, val=None):
        if val is None:
            val = self.cnt[sem]
        if val <= 0:
            return
        self._emit(eng, lambda e, sem=sem, val=val: e.wait_ge(self.sems[sem], val))

    def drain(self, eng):
        self._emit(eng, lambda e: e.drain())

    def inc(self, eng, sem, k=1):
        self.drain(eng)
        self.cnt[sem] += k
        self._emit(eng, lambda e, sem=sem, k=k: e.sem_inc(self.sems[sem], k))
        return self.cnt[sem]

    def qdma(self, queue):
        if queue == "sp":
            for sname in ("d_sp", "d_sp2"):
                self.wait("sp", sname)
        else:
            for sname in ("d_gp", "d_gp2"):
                self.wait("gp", sname)

    def barrier(self):
        for sname in ("d_sp", "d_sp2", "d_spb"):
            self.wait("sp", sname)
        for sname in ("d_gp", "d_gp2", "d_gpb"):
            self.wait("gp", sname)
        snap = {}
        for e in ENGINES:
            snap[e] = self.inc(e, BAR[e])
        for e in ENGINES:
            for o in ENGINES:
                if o != e:
                    self.wait(e, BAR[o], snap[o])



def dsem(queue, b):
    return {"sp": ("d_sp", "d_sp2"), "gp": ("d_gp", "d_gp2")}[queue][b % 2]

def cdiv(a, b):
    return (a + b - 1) // b


class KB:
    def __init__(self, T=2048, stages=("mamba", "prep", "ffn1", "attn", "ffn2")):
        self.T = T
        self.TQ = T // 4
        self.TH = T // 2
        self.stages = stages
        self.nc = bass.Bass("TRN2", target_bir_lowering=False)
        self.r = Rec()
        self.dram = {}
        self._declare_dram()

    def _declare_dram(self):
        nc, T = self.nc, self.T
        def din(name, shape, dt=F32):
            self.dram[name] = nc.dram_tensor(name, shape, dt,
                                             kind="ExternalInput").ap()
        din("x_fm", [D, T])
        din("in_proj_w", [D, 2 * DI])
        din("conv_w", [4, DI])
        din("conv_b", [DI])
        din("x_proj_w", [DI, DTR + 2 * NST])
        din("dt_proj_w", [DTR, DI])
        din("dt_proj_b", [DI])
        din("A_log", [DI, NST])
        din("D_vec", [DI])
        din("out_proj_w", [DI, D])
        din("wqT", [D, D])
        din("wkT", [D, D])
        din("wvT", [D, D])
        din("woT", [D, D])
        din("wopk", [D, D])
        din("wqpk", [128, 8 * D])
        din("wkpk", [128, 8 * D])
        din("wvpk", [128, 8 * D])
        din("bqkv", [3, D])
        din("bsel", [3, 3, 128])
        din("bo", [D])
        din("w1", [4 * D, D])
        din("w1T", [D, 4 * D])
        din("w2", [D, 4 * D])
        din("w2T", [4 * D, D])
        din("w1pk", [4 * D, D])
        din("w2pk", [D, 4 * D])
        din("ip_pk", [2 * DI, D])
        din("op_pk", [D, DI])
        din("b1", [4 * D])
        din("b2", [D])
        din("ident_in", [128, 128])
        self.dram["out_fm"] = nc.dram_tensor("out_fm", [D, T], F32,
                                             kind="ExternalOutput").ap()
        self.dram["bc_hbm"] = nc.dram_tensor("bc_hbm", [2 * NST, self.TQ],
                                             BF16, kind="Internal").ap()
        self.dram["w1s"] = nc.dram_tensor("w1s", [4 * D, D], BF16,
                                          kind="Internal").ap()
        self.dram["w2s"] = nc.dram_tensor("w2s", [D, 4 * D], BF16,
                                          kind="Internal").ap()
        self.dram["r_hbm"] = nc.dram_tensor("r_hbm", [2, 512], F32,
                                            kind="Internal").ap()
        self.dram["ip_bf"] = nc.dram_tensor("ip_bf", [2 * DI, D], BF16,
                                            kind="Internal").ap()
        self.dram["op_bf"] = nc.dram_tensor("op_bf", [D, DI], BF16,
                                            kind="Internal").ap()

    def build(self):
        nc, r = self.nc, self.r
        T = self.T
        from contextlib import ExitStack
        es = ExitStack()
        self.X = es.enter_context(nc.sbuf_tensor("X", [128, 8 * T], F32))
        self.Xv = self.X[:].rearrange("p (c t) -> p c t", c=8)
        self.AR = es.enter_context(
            nc.sbuf_tensor("AR", [128, ARENA_BYTES], U8))
        so = [0]
        def sv(shape, dt=F32):
            esz = 2 if dt == BF16 else 4
            n = int(np.prod(shape[1:])) * esz
            o = so[0]
            so[0] += (n + 63) & ~63
            return self.view(o, dt, shape)
        self.convb_t = sv([128, 16])
        self.cw_t = sv([128, 4, 16])
        self.dtb_t = sv([128, 16])
        self.d_t = sv([128, 16])
        self.a_t = sv([128, 256])
        self.bo_t = sv([128, 8])
        self.b1_t = sv([128, 32])
        self.b2_t = sv([128, 8])
        self.al1_t = sv([128, 32])
        self.al2_t = sv([128, 8])
        self.carry = sv([128, 256])
        self.consts = sv([128, 2])
        self.ident = sv([128, 128], BF16)
        self.identb = self.ident
        self.onesf = sv([128, 128], F32)
        self.onesb = sv([128, 128], BF16)
        self.onesrow = sv([1, 128], BF16)
        self.tails = sv([128, 16, 3], BF16)
        assert so[0] <= SBASE, f"static region {so[0]}"
        self.ps2 = [es.enter_context(nc.psum_tensor(f"ps{i}", [128, 1024], F32))
                    for i in range(4)]
        r.sems = {s: es.enter_context(nc.semaphore(s)) for s in SEMS}

        self._setup_consts()
        self._load_x()
        if "mamba" in self.stages:
            self._mamba()
        if "ffn1" in self.stages:
            self._ffn(first=True)
        if "attn" in self.stages:
            self._attn()
        if "ffn2" in self.stages:
            self._ffn(first=False)
        self._store_out()

        with nc.Block() as block:
            @block.sync
            def _(e):
                for f in r.ops["sp"]: f(e)
            @block.gpsimd
            def _(e):
                for f in r.ops["gp"]: f(e)
            @block.tensor
            def _(e):
                for f in r.ops["pe"]: f(e)
            @block.scalar
            def _(e):
                for f in r.ops["act"]: f(e)
            @block.vector
            def _(e):
                for f in r.ops["ve"]: f(e)
        es.close()
        return nc

    def bank(self, i, w=512):
        return self.ps2[i // 2][:, (i % 2) * 512:(i % 2) * 512 + w]

    def pair(self, i, w=1024):
        assert i % 2 == 0
        return self.ps2[i // 2][:, 0:w]

    def view(self, off, dt, shape):
        esz = 2 if dt == BF16 else 4
        n = int(np.prod(shape[1:]))
        ap = self.AR[0:shape[0], off:off + n * esz].bitcast(dt)
        if len(shape) == 3:
            ap = ap.rearrange("p (a b) -> p a b", a=shape[1])
        return ap

    def _setup_consts(self):
        r, d = self.r, self.dram
        def ld(dst, src):
            r.op("sp", lambda e, dst=dst, src=src: e.dma_start(
                out=dst, in_=src, allow_slow_non_contiguous=True),
                 inc="d_spb", inck=16)
        ld(self.convb_t[:], d["conv_b"].rearrange("(c p) -> p c", p=128))
        ld(self.cw_t[:], d["conv_w"].rearrange("j (c p) -> p j c", p=128))
        ld(self.dtb_t[:], d["dt_proj_b"].rearrange("(c p) -> p c", p=128))
        ld(self.d_t[:], d["D_vec"].rearrange("(c p) -> p c", p=128))
        ld(self.a_t[:].rearrange("p (c n) -> p c n", c=16),
           d["A_log"].rearrange("(c p) n -> p c n", p=128))
        ld(self.bo_t[:], d["bo"].rearrange("(c p) -> p c", p=128))
        ld(self.b1_t[:], d["b1"].rearrange("(c p) -> p c", p=128))
        ld(self.b2_t[:], d["b2"].rearrange("(c p) -> p c", p=128))
        ldc = r.cnt["d_spb"]
        r.wait("sp", "d_spb", ldc)
        r.op("ve", lambda e: e.memset(self.consts[:, 0:1], 1e-5))
        r.op("ve", lambda e: e.memset(self.consts[:, 1:2], -10.0))
        r.op("ve", lambda e: e.memset(self.onesf[:], 1.0 / 1024.0))
        r.op("ve", lambda e: e.memset(self.onesb[:], 1.0 / 1024.0))
        r.op("ve", lambda e: e.memset(self.onesrow[:], 1.0))
        r.op("ve", lambda e: e.memset(self.carry[:], 0.0))
        r.drain("ve")
        r.op("gp", lambda e: e.dma_start(out=self.ident,
                                         in_=d["ident_in"]),
             inc="d_gpb", inck=16)
        r.wait("act", "d_spb", ldc)
        r.op("act", lambda e: e.activation(out=self.a_t[:], in_=self.a_t[:],
                                           func=AF.Exp))
        r.drain("act")
        r.op("act", lambda e: e.activation(out=self.a_t[:], in_=self.a_t[:],
                                           func=AF.Copy, scale=-1.0),
             inc="c_act")
        r.wait("gp", "d_gpb")
        r.barrier()

    def _load_x(self):
        r = self.r
        r.op("sp", lambda e: e.dma_start(
            out=self.Xv,
            in_=self.dram["x_fm"].rearrange("(c p) t -> p c t", p=128)),
             inc="d_spb", inck=16)
        c = self.r.cnt["d_spb"]
        for e in ("ve", "pe", "act", "gp"):
            r.wait(e, "d_spb", c)

    def _layernorm(self, t0, tlen, xn_view, o_ln, bb=0, bb2=2, waits=None):
        r = self.r
        tsl = slice(t0, t0 + tlen)
        xsq = self.view(o_ln, BF16, [128, 8, tlen])
        mean_sb = self.view(o_ln + 8 * tlen * 2, F32, [128, tlen])
        rstd_sb = self.view(o_ln + 8 * tlen * 2 + 4 * tlen, F32, [128, tlen])
        t1 = self.view(o_ln, F32, [128, tlen])
        nsl = cdiv(tlen, 512)
        if waits is None:
            r.wait("act", "c_ve")
        else:
            for eng, sem, val in waits:
                r.wait(eng, sem, val)
        r.op("act", lambda e: e.activation(out=xsq, in_=self.Xv[:, :, tsl],
                                           func=AF.Square), inc="c_act")
        if waits is None:
            r.wait("pe", "c_act")
            r.wait("pe", "c_ve")
        else:
            r.wait("pe", "c_act")
        for ns in range(nsl):
            a, b = ns * 512, min((ns + 1) * 512, tlen)
            for k in range(8):
                r.op("pe", lambda e, k=k, ns=ns, a=a, b=b: e.matmul(
                    self.bank(bb + ns, b - a), self.onesf[:],
                    self.Xv[:, k, t0 + a:t0 + b], start=(k == 0),
                    stop=(k == 7)))
            for k in range(8):
                r.op("pe", lambda e, k=k, ns=ns, a=a, b=b: e.matmul(
                    self.bank(bb2 + ns, b - a), self.onesb[:],
                    xsq[:, k, a:b], start=(k == 0), stop=(k == 7)),
                     inc="c_pe" if (k == 7 and ns == nsl - 1) else None)
        r.wait("act", "c_pe")
        r.op("act", lambda e: e.activation(
            out=mean_sb, in_=self.bank(bb, tlen) if tlen <= 512 else
            self.pair(bb, tlen), func=AF.Copy), inc="c_act")
        r.wait("ve", "c_act")
        r.op("ve", lambda e: e.tensor_tensor(out=rstd_sb, in0=mean_sb,
                                             in1=mean_sb, op=ALU.mult))
        r.drain("ve")
        r.op("ve", lambda e: e.tensor_tensor(
            out=rstd_sb,
            in0=self.bank(bb2, tlen) if tlen <= 512 else self.pair(bb2, tlen),
            in1=rstd_sb, op=ALU.subtract), inc="c_ve")
        r.wait("act", "c_ve")
        r.op("act", lambda e: e.activation(out=rstd_sb, in_=rstd_sb,
                                           func=AF.Ln,
                                           bias=self.consts[:, 0:1]))
        r.drain("act")
        r.op("act", lambda e: e.activation(out=rstd_sb, in_=rstd_sb,
                                           func=AF.Exp, scale=-0.5),
             inc="c_act")
        r.wait("ve", "c_act")
        for c in range(8):
            r.op("ve", lambda e, c=c: e.tensor_tensor(
                out=t1, in0=self.Xv[:, c, tsl], in1=mean_sb,
                op=ALU.subtract))
            r.drain("ve")
            r.op("ve", lambda e, c=c: e.tensor_tensor(
                out=xn_view[:, c, :], in0=t1, in1=rstd_sb, op=ALU.mult),
                 inc="c_ve" if c == 7 else None)
            r.drain("ve")

    def _mamba(self):
        T, TQ, r, d = self.T, self.TQ, self.r, self.dram
        off = [SBASE]
        def alloc(n):
            n = (n + 63) & ~63
            o = off[0]; off[0] += n; return o
        o_xn = alloc(8 * TQ * 2)
        o_xbpre = alloc(16 * (TQ + 3) * 2 + 2)
        o_xbs = alloc(16 * TQ * 2)
        o_zbs = alloc(16 * TQ * 2)
        o_bbc = alloc(16 * TQ * 2)
        o_cbc = alloc(16 * TQ * 2)
        o_dtraw = alloc(TQ * 2)
        o_bc = o_cbc
        o_dt = alloc(2 * TQ * 4)
        o_vdup = alloc(2 * 2 * TQ * 2)
        o_dec = alloc(2 * 2 * TQ * 4)
        o_inp = alloc(2 * 2 * TQ * 2)
        o_h = alloc(2 * 2 * TQ * 2)
        o_hc = alloc(2 * 2 * TQ * 2)
        o_seed = alloc(320)
        o_yv = alloc(2 * TQ * 4)
        o_cacc = o_yv
        o_wi = alloc(2 * 8 * 128 * 2)
        if 16 * TQ * 2 >= 2 * 16 * 128 * 2:
            o_wo = o_cbc
        else:
            o_wo = alloc(2 * 16 * 128 * 2)
        o_xpw = alloc(16 * 160 * 2)
        o_dtw = alloc(16 * 128 * 2)
        o_lnm = o_xbpre
        assert off[0] <= ARENA_BYTES, f"mamba arena {off[0]}"

        xn = self.view(o_xn, BF16, [128, 8, TQ])
        xbpre = self.view(o_xbpre, BF16, [128, 16, TQ + 3])
        xbs = self.view(o_xbs, BF16, [128, 16, TQ])
        zbs = self.view(o_zbs, BF16, [128, 16, TQ])
        bbc = self.view(o_bbc, BF16, [128, 16, TQ])
        cbc = self.view(o_cbc, BF16, [128, 16, TQ])
        dtraw = self.view(o_dtraw, BF16, [128, TQ])
        bcrows = self.view(o_bc, BF16, [128, TQ])
        dt = [self.view(o_dt + i * TQ * 4, F32, [128, TQ]) for i in (0, 1)]
        vdup = [self.view(o_vdup + i * 2 * TQ * 2, BF16, [128, 2, TQ])
                for i in (0, 1)]
        dec_all = [self.view(o_dec + (i % 2) * 2 * TQ * 4, F32, [128, 2, TQ])
                   for i in range(3)]
        inp_all = [self.view(o_inp + i * 2 * TQ * 2, BF16, [128, 2, TQ])
                   for i in (0, 1)]
        h_all = [self.view(o_h + i * 2 * TQ * 2, BF16, [128, 2, TQ])
                 for i in (0, 1)]
        hc_all = [self.view(o_hc + i * 2 * TQ * 2, BF16, [128, 2, TQ])
                  for i in (0, 1)]
        dec0s = [self.view(o_seed + i * 64, F32, [128, 16]) for i in (0, 1)]
        seedv = [self.view(o_seed + 128 + i * 64, F32, [128, 16])
                 for i in (0, 1)]
        yv = [self.view(o_yv + i * TQ * 2, BF16, [128, TQ]) for i in (0, 1)]
        cacc = [self.view(o_cacc + i * TQ * 4, F32, [128, TQ]) for i in (0, 1)]
        wi = [self.view(o_wi + i * 8 * 128 * 2, BF16, [128, 8, 128])
              for i in (0, 1)]
        wo = [self.view(o_wo + i * 16 * 128 * 2, BF16, [128, 16, 128])
              for i in (0, 1)]
        xpw = self.view(o_xpw, BF16, [128, 16, 160])
        dtw = self.view(o_dtw, BF16, [128, 16, 128])
        av = self.a_t[:].rearrange("p (c n) -> p c n", c=16)

        for i in range(2):
            r.op("ve", lambda e, i=i: e.memset(dec_all[i][:, :, 0:1], 0.0))
        r.inc("ve", "c_ve")
        r.op("gp", lambda e: e.dma_start(
            out=xpw, in_=d["x_proj_w"].rearrange("(k p) f -> p k f", p=128)),
             inc="d_gpb", inck=16)
        r.op("gp", lambda e: e.dma_start(
            out=dtw.rearrange("p a b -> p (a b)"), in_=d["dt_proj_w"]),
             inc="d_gpb", inck=16)
        small_w = r.cnt["d_gpb"]
        r.wait("gp", "d_gpb", small_w)
        r.wait("pe", "d_gpb", small_w)

        st = {}

        def do_ln(qq):
            lv = max(st.get("ln_ve", 0), st.get("conv_ve", 0))
            ln_waits = [("act", "c_ve", lv), ("pe", "c_ve", lv)]
            self._layernorm(qq * TQ, TQ, xn, o_lnm, bb=4, bb2=5,
                            waits=ln_waits)
            st["xn_ready"] = r.cnt["c_ve"]
            st["ln_ve"] = r.cnt["c_ve"]
            st["ln_evac"] = r.cnt["c_act"]

        def inproj_block(qq, m, mm_done, ev_done, wr_done, dst):
            b = m % 2
            if qq == 0:
                if m >= 2:
                    r.wait("gp", "c_pe", mm_done[m - 2])
                    r.wait("gp", dsem("sp", b), wr_done[m - 2])
                ddone = r.op("gp", lambda e, m=m, b=b: e.dma_start(
                    out=wi[b].rearrange("p a b -> p (a b)"),
                    in_=d["ip_pk"][m * 128:(m + 1) * 128, :]),
                     inc=dsem("gp", b), inck=16)[dsem("gp", b)]
                r.wait("sp", dsem("gp", b), ddone)
                wr_done[m] = r.op("sp", lambda e, m=m, b=b: e.dma_start(
                    out=d["ip_bf"][m * 128:(m + 1) * 128, :],
                    in_=wi[b].rearrange("p a b -> p (a b)")),
                     inc=dsem("sp", b), inck=16)[dsem("sp", b)]
                st["sp_par"][b] = wr_done[m]
                r.wait("pe", dsem("gp", b), ddone)
            else:
                if m >= 2:
                    r.wait("sp", "c_pe", mm_done[m - 2])
                if m in (0, 1, 16, 17):
                    r.wait("sp", dsem("sp", b), st["sp_par"][b])
                ddone = r.op("sp", lambda e, m=m, b=b: e.dma_start(
                    out=wi[b].rearrange("p a b -> p (a b)"),
                    in_=d["ip_bf"][m * 128:(m + 1) * 128, :]),
                     inc=dsem("sp", b), inck=16)[dsem("sp", b)]
                st["sp_par"][b] = ddone
                r.wait("pe", dsem("sp", b), ddone)
            r.wait("pe", "c_ve", st["xn_ready"])
            if m >= 2:
                r.wait("pe", "c_act", ev_done[m - 2])
            else:
                r.wait("pe", "c_act", st["ln_evac"])
                r.wait("pe", "c_ve", st["ln_var"])
            for k in range(8):
                c = r.op("pe", lambda e, m=m, b=b, k=k: e.matmul(
                    self.bank(4 + b, TQ), wi[b][:, k, :], xn[:, k, :],
                    start=(k == 0), stop=(k == 7)),
                         inc="c_pe" if k == 7 else None)
            mm_done[m] = c["c_pe"]
            r.wait("act", "c_pe", mm_done[m])
            if m < 16:
                c = r.op("act", lambda e, m=m, b=b: e.activation(
                    out=xbpre[:, m, 3:3 + TQ], in_=self.bank(4 + b, TQ),
                    func=AF.Copy), inc="c_act")
            else:
                r.wait("act", "c_pe", st.get("op_mm", 0))
                c = r.op("act", lambda e, m=m, b=b: e.activation(
                    out=zbs[:, m - 16, :], in_=self.bank(4 + b, TQ),
                    func=AF.Silu), inc="c_act")
            ev_done[m] = c["c_act"]
            st["ip_ev"] = c["c_act"]

        def do_conv(qq):
            r.wait("ve", "c_act", st["ip_ev"])
            if qq == 0:
                r.op("ve", lambda e: e.memset(xbpre[:, :, 0:3], 0.0),
                     inc="c_ve")
            else:
                r.op("ve", lambda e: e.tensor_copy(out=xbpre[:, :, 0:3],
                                                   in_=self.tails),
                     inc="c_ve")
            r.drain("ve")
            silu_done = [0] * 16
            for c in range(16):
                b = c % 2
                acc = cacc[b]
                if c >= 2:
                    r.wait("ve", "c_act", silu_done[c - 2])
                r.op("ve", lambda e, c=c, acc=acc: e.tensor_scalar(
                    out=acc, in0=xbpre[:, c, 0:TQ],
                    scalar1=self.cw_t[:, 0, c:c + 1], scalar2=None,
                    op0=ALU.mult))
                r.drain("ve")
                for j in range(1, 4):
                    r.op("ve", lambda e, c=c, j=j, acc=acc:
                         e.scalar_tensor_tensor(
                             out=acc, in0=xbpre[:, c, j:j + TQ],
                             scalar=self.cw_t[:, j, c:c + 1], in1=acc,
                             op0=ALU.mult, op1=ALU.add))
                    r.drain("ve")
                cv = r.inc("ve", "c_ve")
                r.wait("act", "c_ve", cv)
                cc = r.op("act", lambda e, c=c, acc=acc: e.activation(
                    out=xbs[:, c, :], in_=acc, func=AF.Silu,
                    bias=self.convb_t[:, c:c + 1]), inc="c_act")
                silu_done[c] = cc["c_act"]
            r.wait("ve", "c_act", silu_done[15])
            r.op("ve", lambda e: e.tensor_copy(out=self.tails,
                                               in_=xbpre[:, :, TQ:TQ + 3]),
                 inc="c_ve")
            st["xbs_ready"] = silu_done[15]
            st["conv_ve"] = r.cnt["c_ve"]

        def do_xproj(qq):
            r.wait("pe", "c_act", st["xbs_ready"])
            for k in range(16):
                r.op("pe", lambda e, k=k: e.matmul(
                    self.bank(4, TQ), xpw[:, k, 0:128], xbs[:, k, :],
                    start=(k == 0), stop=(k == 15)))
            for k in range(16):
                r.op("pe", lambda e, k=k: e.matmul(
                    self.bank(5, TQ)[0:32, :], xpw[:, k, 128:160],
                    xbs[:, k, :], start=(k == 0), stop=(k == 15)),
                     inc="c_pe" if k == 15 else None)
            r.wait("act", "c_pe")
            r.op("act", lambda e: e.activation(out=dtraw,
                                               in_=self.bank(4, TQ),
                                               func=AF.Copy))
            r.op("act", lambda e: e.activation(out=bcrows[0:32, :],
                                               in_=self.bank(5, TQ)[0:32, :],
                                               func=AF.Copy), inc="c_act")
            st["dtraw_ready"] = r.cnt["c_act"]
            r.wait("sp", "c_act", st["dtraw_ready"])
            r.wait("sp", "c_ve")
            r.wait("sp", "d_spb")
            for sname in ("d_sp", "d_sp2", "d_gp", "d_gp2"):
                r.wait("sp", sname)
            r.op("sp", lambda e: e.dma_start(out=d["bc_hbm"][:, 0:TQ],
                                             in_=bcrows[0:32, :]),
                 inc="d_spb", inck=16)
            r.wait("sp", "d_spb")
            bcall = self.view(o_bbc, BF16, [128, 2 * NST, TQ])
            r.op("sp", lambda e, bcall=bcall: e.dma_start(
                out=bcall,
                in_=d["bc_hbm"][:, 0:TQ].unsqueeze(0)
                .broadcast_to([128, 2 * NST, TQ])),
                 inc="d_spb", inck=16)
            st["bc_done"] = r.cnt["d_spb"]

        def do_outproj(qq, o_state):
            tsl = slice(qq * TQ, (qq + 1) * TQ)
            r.qdma("gp")
            r.wait("sp", "d_spb")
            r.wait("sp", "c_ve", o_state["ep"])
            r.wait("sp", "c_gp")
            r.wait("gp", "c_ve", o_state["ep"])
            r.wait("pe", "c_ve", o_state["ep"])
            o_mm = [0] * 8
            o_ev = [0] * 8
            o_wr = [0] * 8
            for m in range(8):
                b = m % 2
                if m >= 2:
                    r.wait("gp" if qq == 0 else "sp", "c_pe", o_mm[m - 2])
                if qq == 0:
                    if m >= 2:
                        r.wait("gp", dsem("sp", b), o_wr[m - 2])
                    ddone = r.op("gp", lambda e, m=m, b=b: e.dma_start(
                        out=wo[b].rearrange("p a b -> p (a b)"),
                        in_=d["op_pk"][m * 128:(m + 1) * 128, :]),
                         inc=dsem("gp", b), inck=16)[dsem("gp", b)]
                    r.wait("sp", dsem("gp", b), ddone)
                    if m < 2:
                        r.wait("sp", dsem("sp", b), st["sp_par"][b])
                    o_wr[m] = r.op("sp", lambda e, m=m, b=b: e.dma_start(
                        out=d["op_bf"][m * 128:(m + 1) * 128, :],
                        in_=wo[b].rearrange("p a b -> p (a b)")),
                         inc=dsem("sp", b), inck=16)[dsem("sp", b)]
                    st["sp_par"][b] = o_wr[m]
                    r.wait("pe", dsem("gp", b), ddone)
                else:
                    if m < 2:
                        r.wait("sp", dsem("sp", b), st["sp_par"][b])
                    ddone = r.op("sp", lambda e, m=m, b=b: e.dma_start(
                        out=wo[b].rearrange("p a b -> p (a b)"),
                        in_=d["op_bf"][m * 128:(m + 1) * 128, :]),
                         inc=dsem("sp", b), inck=16)[dsem("sp", b)]
                    st["sp_par"][b] = ddone
                    r.wait("pe", dsem("sp", b), ddone)
                if m >= 2:
                    r.wait("pe", "c_ve", o_ev[m - 2])
                for k in range(16):
                    c = r.op("pe", lambda e, m=m, b=b, k=k: e.matmul(
                        self.bank(2 + b, TQ), wo[b][:, k, :], zbs[:, k, :],
                        start=(k == 0), stop=(k == 15)),
                             inc="c_pe" if k == 15 else None)
                o_mm[m] = c["c_pe"]
                r.wait("ve", "c_pe", o_mm[m])
                o_ev[m] = r.op("ve", lambda e, m=m, b=b, tsl=tsl:
                               e.tensor_tensor(
                                   out=self.Xv[:, m, tsl],
                                   in0=self.bank(2 + b, TQ),
                                   in1=self.Xv[:, m, tsl], op=ALU.add),
                               inc="c_ve")["c_ve"]
            st["op_mm"] = o_mm[7]

        def do_scan(qq, pieces):
            dt_ev = [0, 0]
            vv = [0, 0]
            sv_ = [0, 0]
            dec_done = {}
            inp_done = {}
            carry_done = {}
            h_done = {}
            hc_done = {}
            ymm_done = {}
            ep_done = [0]
            G = 2
            NG = NST // G
            bc_done = st["bc_done"]
            dtraw_ready = st["dtraw_ready"]
            xbs_ready = st["xbs_ready"]
            zb_ready = st["ip_ev"]

            def prologue(dd):
                b = dd % 2
                if dd >= 2:
                    r.wait("pe", "c_act", dt_ev[b])
                else:
                    r.wait("pe", "c_act", dtraw_ready)
                cpe = r.op("pe", lambda e, dd=dd, b=b: e.matmul(
                    self.bank(6 + b, TQ), dtw[:, dd, :], dtraw,
                    start=True, stop=True), inc="c_pe")["c_pe"]
                r.wait("act", "c_pe", cpe)
                if dd >= 2:
                    r.wait("act", "c_ve", vv[b])
                    r.wait("act", "c_act", dec_done[(dd - 2, NG - 1)])
                r.op("act", lambda e, dd=dd, b=b: e.activation(
                    out=dt[b], in_=self.bank(6 + b, TQ), func=AF.Exp,
                    bias=self.dtb_t[:, dd:dd + 1]))
                r.drain("act")
                dt_ev[b] = r.op("act", lambda e, b=b: e.activation(
                    out=dt[b], in_=dt[b], func=AF.Ln, bias=1.0),
                                inc="c_act")["c_act"]
                r.drain("act")
                r.wait("ve", "c_act", dt_ev[b])
                r.wait("ve", "c_act", xbs_ready)
                r.wait("ve", "d_spb", bc_done)
                r.op("ve", lambda e, dd=dd, b=b: e.tensor_tensor(
                    out=vdup[b][:, 0, :], in0=dt[b], in1=xbs[:, dd, :],
                    op=ALU.mult))
                r.drain("ve")
                r.op("ve", lambda e, b=b: e.tensor_copy(
                    out=vdup[b][:, 1, :], in_=vdup[b][:, 0, :]))
                r.op("ve", lambda e, b=b, dd=dd: e.tensor_tensor(
                    out=dec0s[b], in0=av[:, dd, :],
                    in1=dt[b][:, 0:1].broadcast_to([128, 16]), op=ALU.mult),
                     inc="c_ve")
                r.drain("ve")
                vv[b] = r.cnt["c_ve"]
                r.wait("act", "c_ve", vv[b])
                r.op("act", lambda e, b=b: e.activation(
                    out=dec0s[b], in_=dec0s[b], func=AF.Exp), inc="c_act")
                d0 = r.cnt["c_act"]
                r.wait("ve", "c_act", d0)
                r.op("ve", lambda e, dd=dd, b=b: e.tensor_tensor(
                    out=seedv[b], in0=dec0s[b],
                    in1=self.carry[:, dd * 16:dd * 16 + 16], op=ALU.mult))
                r.drain("ve")
                r.op("ve", lambda e, b=b: e.tensor_tensor(
                    out=dec0s[b], in0=bbc[:, 0:16, 0],
                    in1=vdup[b][:, 0, 0:1].broadcast_to([128, 16]),
                    op=ALU.mult))
                r.drain("ve")
                sv_[b] = r.op("ve", lambda e, b=b: e.tensor_tensor(
                    out=seedv[b], in0=seedv[b], in1=dec0s[b], op=ALU.add),
                              inc="c_ve")["c_ve"]
                r.drain("ve")

            def groups(dd):
                b = dd % 2
                for ng in range(NG):
                    gb = (dd * NG + ng) % 2
                    hb = ng % 2
                    ib = ng % 2
                    key = (dd, ng)
                    keyp2 = (dd, ng - 2) if ng >= 2 else (dd - 1, NG - 2 + ng)
                    nbase = ng * G
                    idx = dd * 16 + nbase
                    if dd + ng > 0:
                        r.wait("act", "c_ve", h_done.get(keyp2, 0))
                    for n in range(G):
                        r.op("act", lambda e, dd=dd, b=b, gb=gb, n=n,
                             nbase=nbase: e.activation(
                                 out=dec_all[gb][:, n, 1:TQ],
                                 in_=dt[b][:, 1:TQ], func=AF.Exp,
                                 scale=av[:, dd, nbase + n:nbase + n + 1]))
                    r.wait("act", "c_ve", sv_[b])
                    ca = r.op("act", lambda e, b=b, ib=ib, nbase=nbase:
                              e.activation(
                                  out=inp_all[ib][:, :, 0],
                                  in_=seedv[b][:, nbase:nbase + G],
                                  func=AF.Copy), inc="c_act")
                    col0_done = ca["c_act"]
                    dec_done[key] = col0_done
                    r.wait("ve", "d_spb", bc_done)
                    r.op("ve", lambda e, b=b, ib=ib, nbase=nbase:
                         e.tensor_tensor(
                             out=inp_all[ib][:, :, 1:],
                             in0=vdup[b][:, :, 1:],
                             in1=bbc[:, nbase:nbase + G, 1:],
                             op=ALU.mult))
                    r.drain("ve")
                    r.wait("ve", "c_act", col0_done)
                    r.op("ve", lambda e, gb=gb, hb=hb, ib=ib:
                         e.tensor_tensor_scan(
                             out=h_all[hb].rearrange("p a b -> p (a b)"),
                             data0=dec_all[gb].rearrange("p a b -> p (a b)"),
                             data1=inp_all[ib].rearrange("p a b -> p (a b)"),
                             initial=0.0, op0=ALU.mult, op1=ALU.add))
                    hd = r.inc("ve", "c_ve")
                    h_done[key] = hd
                    r.wait("act", "c_ve", hd)
                    carry_done[key] = r.op(
                        "act", lambda e, hb=hb, idx=idx: e.activation(
                            out=self.carry[:, idx:idx + G],
                            in_=h_all[hb][:, :, TQ - 1], func=AF.Copy),
                        inc="c_act")["c_act"]
                    if dd + ng > 0:
                        r.wait("ve", "c_pe", ymm_done.get(keyp2, 0))
                    hc = r.op("ve", lambda e, hb=hb, nbase=nbase:
                              e.tensor_tensor(
                                  out=hc_all[hb], in0=h_all[hb],
                                  in1=cbc[:, nbase:nbase + G, :],
                                  op=ALU.mult), inc="c_ve")["c_ve"]
                    hc_done[key] = ("c_ve", hc)
                    r.wait("pe", "c_ve", hc)
                    for n in range(G):
                        c = r.op("pe", lambda e, b=b, hb=hb, n=n, ng=ng:
                                 e.matmul(
                                     self.bank(b, TQ), self.ident[:],
                                     hc_all[hb][:, n, :],
                                     start=(ng == 0 and n == 0),
                                     stop=(ng == NG - 1 and n == G - 1)),
                                 inc="c_pe" if n == G - 1 else None)
                    ymm_done[key] = c["c_pe"]
                r.wait("ve", "c_pe", ymm_done[(dd, NG - 1)])
                r.wait("ve", "c_act", zb_ready)
                r.op("ve", lambda e, dd=dd, b=b: e.scalar_tensor_tensor(
                    out=yv[b], in0=xbs[:, dd, :],
                    scalar=self.d_t[:, dd:dd + 1], in1=self.bank(b, TQ),
                    op0=ALU.mult, op1=ALU.add))
                r.drain("ve")
                ep_done[0] = r.op("ve", lambda e, dd=dd, b=b: e.tensor_tensor(
                    out=zbs[:, dd, :], in0=yv[b], in1=zbs[:, dd, :],
                    op=ALU.mult), inc="c_ve")["c_ve"]

            for dd in range(17):
                if dd < 16:
                    prologue(dd)
                if dd >= 1:
                    groups(dd - 1)
                for _ in range(2):
                    if pieces:
                        pieces.pop(0)()
            while pieces:
                pieces.pop(0)()
            return ep_done[0]

        st["ln_ve"] = 0
        st["ln_var"] = 0
        st["sp_par"] = [0, 0]
        do_ln(0)
        st["ln_var"] = r.cnt["c_ve"]
        mm0, ev0, wr0 = [0] * 32, [0] * 32, [0] * 32
        for m in range(32):
            inproj_block(0, m, mm0, ev0, wr0, None)
        do_conv(0)
        do_xproj(0)
        for q in range(4):
            pieces = []
            if q < 3:
                qq = q + 1
                mmw, evw, wrw = [0] * 32, [0] * 32, [0] * 32
                def mk_ln(qq=qq):
                    def f():
                        do_ln(qq)
                        st["ln_var"] = r.cnt["c_ve"]
                    return f
                pieces.append(mk_ln())
                for m in range(16):
                    def mk_ip(qq=qq, m=m, mmw=mmw, evw=evw, wrw=wrw):
                        return lambda: inproj_block(qq, m, mmw, evw, wrw, None)
                    pieces.append(mk_ip())
            ep = do_scan(q, pieces)
            do_outproj(q, {"ep": ep})
            if q < 3:
                qq = q + 1
                for m in range(16, 32):
                    inproj_block(qq, m, mmw, evw, wrw, None)
                do_conv(qq)
                do_xproj(qq)
        r.barrier()


    def _ffn_prep(self):
        r, d = self.r, self.dram
        o_st = SBASE
        red = self.view(o_st + 2 * 4096 * 4, F32, [128, 1])
        red_done = [0, 0]
        for wname, aout, nb, nw, inv in (
                ("w1", self.al1_t, 32, 1024, 1.0 / 1024.0),
                ("w2", self.al2_t, 8, 4096, 1.0 / 4096.0)):
            stage = [self.view(o_st + i * nw * 4, F32, [128, nw])
                     for i in (0, 1)]
            red_done = [0, 0]
            for m in range(nb):
                b = m % 2
                if m >= 2:
                    r.wait("sp", "c_ve", red_done[b])
                else:
                    r.wait("sp", "c_ve")
                ddone = r.op("sp", lambda e, m=m, b=b, wname=wname,
                             stage=stage:
                     e.dma_start(out=stage[b],
                                 in_=d[wname][m * 128:(m + 1) * 128, :]),
                     inc=dsem("sp", b), inck=16)[dsem("sp", b)]
                r.wait("ve", dsem("sp", b), ddone)
                r.op("ve", lambda e, b=b, stage=stage: e.reduce_sum(
                    out=red, in_=stage[b], axis=AX.X,
                    apply_absolute_value=True))
                r.drain("ve")
                red_done[b] = r.op(
                    "ve", lambda e, m=m, aout=aout, inv=inv: e.tensor_scalar(
                        out=aout[:, m:m + 1], in0=red, scalar1=inv,
                        scalar2=None, op0=ALU.mult), inc="c_ve")["c_ve"]
                r.drain("ve")
        o_sg = o_st + 2 * 4096 * 4 + 64
        sgi = [self.view(o_sg + i * 32 * 128 * 4, F32, [128, 32, 128])
               for i in (0, 1)]
        sgo = [self.view(o_sg + 2 * 32 * 128 * 4 + i * 32 * 128 * 2, BF16,
                         [128, 32, 128]) for i in (0, 1)]
        r.wait("sp", "c_ve")
        st_done = [0, 0]
        sg_done = [0, 0]
        for wname, sname, nk, nb in (("w1T", "w1s", 8, 32),
                                     ("w2T", "w2s", 32, 8)):
            for m in range(nb):
                b = m % 2
                if st_done[b]:
                    r.wait("sp", dsem("sp", b), st_done[b])
                    r.wait("sp", "c_act", sg_done[b])
                ddone = r.op("sp", lambda e, m=m, b=b, wname=wname, nk=nk:
                     e.dma_start(out=sgi[b][:, 0:nk, :],
                                 in_=d[wname][:, m * 128:(m + 1) * 128]
                                 .rearrange("(k p) f -> p k f", p=128)),
                     inc=dsem("sp", b), inck=16)[dsem("sp", b)]
                r.wait("act", dsem("sp", b), ddone)
                if st_done[b]:
                    r.wait("act", dsem("sp", b), st_done[b])
                r.op("act", lambda e, b=b, nk=nk: e.sign(
                    out=sgo[b][:, 0:nk, :], in_=sgi[b][:, 0:nk, :]))
                sg_done[b] = r.inc("act", "c_act")
                r.wait("sp", "c_act", sg_done[b])
                st_done[b] = r.op(
                    "sp", lambda e, m=m, b=b, sname=sname, nk=nk:
                    e.dma_start(out=d[sname][:, m * 128:(m + 1) * 128]
                                .rearrange("(k p) f -> p k f", p=128),
                                in_=sgo[b][:, 0:nk, :]),
                    inc=dsem("sp", b), inck=16)[dsem("sp", b)]
        r.barrier()

    def _ffn(self, first=False):
        T, TH, r, d = self.T, self.TH, self.r, self.dram
        off = [SBASE]
        def alloc(n):
            n = (n + 63) & ~63
            o = off[0]; off[0] += n; return o
        o_xn = alloc(8 * TH * 2)
        o_h = alloc(32 * TH * 2)
        o_w1 = alloc(2 * 8 * 128 * 2)
        o_w2 = alloc(2 * 16 * 128 * 2)
        o_g1 = alloc(2 * 8 * 128 * 4)
        o_g2 = alloc(2 * 16 * 128 * 4)
        o_tmp = alloc(2 * TH * 4)
        o_ln = o_h
        assert off[0] <= ARENA_BYTES, f"ffn arena {off[0]}"
        xn = self.view(o_xn, BF16, [128, 8, TH])
        hact = self.view(o_h, BF16, [128, 32, TH])
        w1b = [self.view(o_w1 + i * 8 * 128 * 2, BF16, [128, 8, 128])
               for i in (0, 1)]
        w2h = [self.view(o_w2 + i * 16 * 128 * 2, BF16, [128, 16, 128])
               for i in (0, 1)]
        sg1 = [self.view(o_g1 + i * 8 * 128 * 4, F32, [128, 8, 128])
               for i in (0, 1)]
        sg2 = [self.view(o_g2 + i * 16 * 128 * 4, F32, [128, 16, 128])
               for i in (0, 1)]
        tmp = [self.view(o_tmp + i * TH * 4, F32, [128, TH]) for i in (0, 1)]
        nsl = cdiv(TH, 512)

        for hh in range(2):
            prep = first and hh == 0
            t0 = hh * TH
            tsl = slice(t0, t0 + TH)
            self._layernorm(t0, TH, xn, o_ln)
            r.wait("pe", "c_ve")
            r.qdma("sp")
            mm_done = [0] * 32
            ev_done = [0] * 32
            st_done = [0, 0]
            ab_done = [0, 0]
            for m in range(32):
                b = m % 2
                if m >= 2:
                    r.wait("sp", "c_pe", mm_done[m - 2])
                if prep:
                    if st_done[b]:
                        r.wait("sp", dsem("sp", b), st_done[b])
                    ddone = r.op("sp", lambda e, m=m, b=b: e.dma_start(
                        out=sg1[b].rearrange("p a b -> p (a b)"),
                        in_=d["w1pk"][m * 128:(m + 1) * 128, :]),
                         inc=dsem("sp", b), inck=16)[dsem("sp", b)]
                    r.wait("act", dsem("sp", b), ddone)
                    if st_done[b]:
                        r.wait("act", dsem("sp", b), st_done[b])
                    r.op("act", lambda e, b=b: e.sign(out=w1b[b],
                                                      in_=sg1[b]))
                    r.drain("act")
                    r.op("act", lambda e, b=b: e.activation(
                        out=sg1[b], in_=sg1[b], func=AF.Abs))
                    ab_done[b] = r.inc("act", "c_act")
                    r.wait("pe", "c_act", ab_done[b])
                    for k in range(8):
                        r.op("pe", lambda e, m=m, b=b, k=k: e.matmul(
                            self.bank(0, 512)[:, m:m + 1], sg1[b][:, k, :],
                            self.onesf[:, 0:1], start=(k == 0),
                            stop=(k == 7)))
                    r.wait("sp", "c_act", ab_done[b])
                    st_done[b] = r.op(
                        "sp", lambda e, m=m, b=b: e.dma_start(
                            out=d["w1s"][m * 128:(m + 1) * 128, :],
                            in_=w1b[b].rearrange("p a b -> p (a b)")),
                        inc=dsem("sp", b), inck=16)[dsem("sp", b)]
                else:
                    ddone = r.op("sp", lambda e, m=m, b=b: e.dma_start(
                        out=w1b[b].rearrange("p a b -> p (a b)"),
                        in_=d["w1s"][m * 128:(m + 1) * 128, :]),
                         inc=dsem("sp", b), inck=16)[dsem("sp", b)]
                    r.wait("pe", dsem("sp", b), ddone)
                if m >= 2:
                    r.wait("pe", "c_act", ev_done[m - 2])
                for ns in range(nsl):
                    a, bnd = ns * 512, min((ns + 1) * 512, TH)
                    for k in range(8):
                        c = r.op("pe", lambda e, m=m, b=b, k=k, a=a, bnd=bnd:
                                 e.matmul(
                                     self.pair(4 + 2 * b, TH)[:, a:bnd],
                                     w1b[b][:, k, :], xn[:, k, a:bnd],
                                     start=(k == 0), stop=(k == 7)),
                                 inc="c_pe" if k == 7 and bnd == TH else None)
                mm_done[m] = c["c_pe"]
                r.wait("act", "c_pe", mm_done[m])
                if prep:
                    r.op("act", lambda e, m=m: e.activation(
                        out=self.al1_t[:, m:m + 1],
                        in_=self.bank(0, 512)[:, m:m + 1], func=AF.Copy))
                    r.drain("act")
                ev_done[m] = r.op("act", lambda e, m=m, b=b: e.activation(
                    out=hact[:, m, :], in_=self.pair(4 + 2 * b, TH),
                    func=AF.Gelu, scale=self.al1_t[:, m:m + 1],
                    bias=self.b1_t[:, m:m + 1]), inc="c_act")["c_act"]
            r.qdma("sp")
            r.wait("pe", "c_act")
            m2_done = [0] * 16
            e2_done = [0] * 8
            al2ev = [0] * 8
            st2 = [0, 0]
            for m in range(8):
                bm = m % 2
                for h in range(2):
                    sb = 2 * m + h
                    b2 = sb % 2
                    k0 = 16 * h
                    if sb >= 2:
                        r.wait("sp", "c_pe", m2_done[sb - 2])
                    if prep:
                        if st2[b2]:
                            r.wait("sp", dsem("sp", b2), st2[b2])
                        ddone = r.op("sp", lambda e, m=m, b2=b2, k0=k0:
                                     e.dma_start(
                                         out=sg2[b2]
                                         .rearrange("p a b -> p (a b)"),
                                         in_=d["w2pk"][
                                             m * 128:(m + 1) * 128,
                                             k0 * 128:(k0 + 16) * 128]),
                             inc=dsem("sp", b2), inck=16)[dsem("sp", b2)]
                        r.wait("act", dsem("sp", b2), ddone)
                        if st2[b2]:
                            r.wait("act", dsem("sp", b2), st2[b2])
                        if sb >= 2:
                            r.wait("act", "c_pe", m2_done[sb - 2])
                        r.op("act", lambda e, b2=b2: e.sign(out=w2h[b2],
                                                            in_=sg2[b2]))
                        r.drain("act")
                        r.op("act", lambda e, b2=b2: e.activation(
                            out=sg2[b2], in_=sg2[b2], func=AF.Abs))
                        ab2 = r.inc("act", "c_act")
                        r.wait("pe", "c_act", ab2)
                        for k in range(16):
                            r.op("pe", lambda e, m=m, b2=b2, k=k, h=h:
                                 e.matmul(
                                     self.bank(4, 512)[:, m:m + 1],
                                     sg2[b2][:, k, :], self.onesf[:, 0:1],
                                     start=(h == 0 and k == 0),
                                     stop=(h == 1 and k == 15)))
                        r.wait("sp", "c_act", ab2)
                        st2[b2] = r.op(
                            "sp", lambda e, m=m, b2=b2, k0=k0: e.dma_start(
                                out=d["w2s"][m * 128:(m + 1) * 128,
                                             k0 * 128:(k0 + 16) * 128],
                                in_=w2h[b2].rearrange("p a b -> p (a b)")),
                            inc=dsem("sp", b2), inck=16)[dsem("sp", b2)]
                    else:
                        ddone = r.op("sp", lambda e, m=m, b2=b2, k0=k0:
                                     e.dma_start(
                                         out=w2h[b2]
                                         .rearrange("p a b -> p (a b)"),
                                         in_=d["w2s"][
                                             m * 128:(m + 1) * 128,
                                             k0 * 128:(k0 + 16) * 128]),
                             inc=dsem("sp", b2), inck=16)[dsem("sp", b2)]
                        r.wait("pe", dsem("sp", b2), ddone)
                    if h == 0 and m >= 2:
                        r.wait("pe", "c_ve", e2_done[m - 2])
                    for ns in range(nsl):
                        a, bnd = ns * 512, min((ns + 1) * 512, TH)
                        for k in range(16):
                            c = r.op("pe", lambda e, m=m, bm=bm, b2=b2, k=k,
                                     k0=k0, h=h, a=a, bnd=bnd: e.matmul(
                                         self.pair(0 + 2 * bm, TH)[:, a:bnd],
                                         w2h[b2][:, k, :],
                                         hact[:, k0 + k, a:bnd],
                                         start=(h == 0 and k == 0),
                                         stop=(h == 1 and k == 15)),
                                     inc="c_pe" if k == 15 and bnd == TH
                                     else None)
                    m2_done[sb] = c["c_pe"]
                r.wait("ve", "c_pe", m2_done[2 * m + 1])
                if prep:
                    r.wait("act", "c_pe", m2_done[2 * m + 1])
                    al2ev[m] = r.op("act", lambda e, m=m: e.activation(
                        out=self.al2_t[:, m:m + 1],
                        in_=self.bank(4, 512)[:, m:m + 1], func=AF.Copy,
                        scale=0.25), inc="c_act")["c_act"]
                    r.wait("ve", "c_act", al2ev[m])
                r.op("ve", lambda e, m=m, bm=bm: e.tensor_scalar(
                    out=tmp[bm], in0=self.pair(0 + 2 * bm, TH),
                    scalar1=self.al2_t[:, m:m + 1],
                    scalar2=self.b2_t[:, m:m + 1],
                    op0=ALU.mult, op1=ALU.add))
                r.drain("ve")
                e2_done[m] = r.op(
                    "ve", lambda e, m=m, bm=bm, tsl=tsl: e.tensor_tensor(
                        out=self.Xv[:, m, tsl], in0=tmp[bm],
                        in1=self.Xv[:, m, tsl], op=ALU.add),
                    inc="c_ve")["c_ve"]
            r.barrier()


    def _attn(self):
        T, r, d = self.T, self.r, self.dram
        U = T // 16
        assert U <= 128
        off = [SBASE]
        def alloc(n):
            n = (n + 63) & ~63
            o = off[0]; off[0] += n; return o
        o_xn = alloc(8 * T * 2)
        o_scr = alloc(8 * T * 2)
        o_wq = alloc(8 * 1024 * 2)
        o_wk = alloc(8 * 1024 * 2)
        o_wv = alloc(8 * 1024 * 2)
        o_qm = alloc(2 * 1024 * 2)
        o_km = alloc(2 * 1024 * 2)
        o_vm = alloc(2 * 1024 * 2)
        o_s = alloc(2 * 256 * 4)
        o_a = alloc(2 * 256 * 2)
        o_sm = alloc(2 * 64)
        o_ot = alloc(2 * 2048 * 2)
        o_brow = alloc(2 * 8 * 128 * 2)
        o_bsel = o_brow + 2048
        o_wb = o_brow
        lnsz = 8 * min(1024, T) * 2 + 8 * min(1024, T)
        o_ln = o_scr if lnsz <= 8 * T * 2 else alloc(lnsz)
        assert off[0] <= ARENA_BYTES, f"attn arena {off[0]}"
        xn = self.view(o_xn, BF16, [128, 8, T])
        scr = self.view(o_scr, BF16, [128, 8, T])
        wq = self.view(o_wq, BF16, [128, 8, 1024])
        wk = self.view(o_wk, BF16, [128, 8, 1024])
        wv = self.view(o_wv, BF16, [128, 8, 1024])
        qm = [self.view(o_qm + i * 2048, BF16, [128, 16, 64]) for i in (0, 1)]
        km = [self.view(o_km + i * 2048, BF16, [128, 16, 64]) for i in (0, 1)]
        vm = [self.view(o_vm + i * 2048, BF16, [128, 16, 64]) for i in (0, 1)]
        sc = [self.view(o_s + i * 1024, F32, [128, 16, 16]) for i in (0, 1)]
        am = [self.view(o_a + i * 512, BF16, [128, 16, 16]) for i in (0, 1)]
        sm = [self.view(o_sm + i * 64, F32, [128, 16]) for i in (0, 1)]
        t1 = qm
        ot = [self.view(o_ot + i * 4096, BF16, [128, 2 * 16, 64])
              .rearrange("p (a b) c -> p a b c", a=2) for i in (0, 1)]
        brow3 = self.view(o_brow, BF16, [3, 1024])
        bsel = self.view(o_bsel, BF16, [3, 3, 128])
        wb = [self.view(o_wb + i * 8 * 128 * 2, BF16, [128, 8, 128])
              for i in (0, 1)]

        for hh in range(cdiv(T, 1024)):
            tl = min(1024, T - hh * 1024)
            self._layernorm(hh * 1024, tl,
                            xn[:, :, hh * 1024:hh * 1024 + tl], o_ln)
        r.qdma("gp")
        for i, wname in enumerate(("wqpk", "wkpk", "wvpk")):
            r.op("gp", lambda e, i=i, wname=wname, wq=wq, wk=wk, wv=wv:
                 e.dma_start(out=(wq, wk, wv)[i]
                             .rearrange("p a b -> p (a b)"),
                             in_=d[wname]),
                 inc="d_gpb", inck=16)
        r.op("gp", lambda e: e.dma_start(out=brow3, in_=d["bqkv"]),
             inc="d_gpb", inck=16)
        r.op("gp", lambda e: e.dma_start(out=bsel, in_=d["bsel"]),
             inc="d_gpb", inck=16)
        wdone = r.cnt["d_gpb"]
        r.wait("gp", "d_gpb", wdone)
        r.wait("pe", "d_gpb", wdone)
        r.wait("pe", "c_ve")
        proj_ev = {}
        sc_red = {}
        a_done = {}
        t1_done = {}
        osum_done = {}
        tr_done = {}
        ev_done = {}
        gpred_done = {}
        for m in range(16):
            b = m % 2
            for wi_, (wt, dst) in enumerate(((wq, qm), (wk, km), (wv, vm))):
                prev = proj_ev.get("last")
                if prev:
                    r.wait("pe", "c_act", prev)
                if wi_ == 0 and m >= 2:
                    r.wait("act", "c_pe", tr_done.get(m - 2, 0))
                    r.wait("pe", "c_pe", tr_done.get(m - 2, 0))
                for ns in range(2):
                    for k in range(8):
                        r.op("pe", lambda e, m=m, k=k, ns=ns, wt=wt:
                             e.matmul(
                                 self.pair(4)[0:U, ns * 512:(ns + 1) * 512],
                                 xn[:, k, m::16],
                                 wt[:, k, ns * 512:(ns + 1) * 512],
                                 start=(k == 0), stop=False))
                    c = r.op("pe", lambda e, wi_=wi_, ns=ns: e.matmul(
                        self.pair(4)[0:U, ns * 512:(ns + 1) * 512],
                        bsel[:, wi_, 0:U],
                        brow3[:, ns * 512:(ns + 1) * 512],
                        start=False, stop=True),
                             inc="c_pe" if ns == 1 else None)
                r.wait("act", "c_pe", c["c_pe"])
                cc = r.op("act", lambda e, b=b, wi_=wi_, dst=dst:
                          e.activation(
                              out=dst[b].rearrange("p a b -> p (a b)")[0:U],
                              in_=self.pair(4)[0:U, :],
                              func=AF.Copy,
                              scale=0.125 if wi_ == 0 else 1.0),
                          inc="c_act")
                proj_ev[(m, wi_)] = cc["c_act"]
                proj_ev["last"] = cc["c_act"]
            r.wait("ve", "c_act", proj_ev[(m, 1)])
            if m >= 2:
                r.wait("ve", "c_ve", a_done.get(m - 2, 0))
            for jg in range(8):
                jb2 = jg % 2
                r.op("ve", lambda e, b=b, jg=jg, jb2=jb2: e.tensor_tensor(
                    out=ot[jb2][0:U],
                    in0=qm[b][0:U].unsqueeze(1).broadcast_to([U, 2, 16, 64]),
                    in1=km[b][0:U, 2 * jg:2 * jg + 2, :].unsqueeze(2)
                    .broadcast_to([U, 2, 16, 64]),
                    op=ALU.mult))
                r.drain("ve")
                cv = r.op("ve", lambda e, b=b, jg=jg, jb2=jb2: e.reduce_sum(
                    out=sc[b][0:U, :, 2 * jg:2 * jg + 2].transpose([0, 2, 1]),
                    in_=ot[jb2][0:U], axis=AX.X), inc="c_ve")["c_ve"]
            sc_red[m] = cv
            r.wait("act", "c_ve", sc_red[m])
            r.op("act", lambda e, b=b: e.activation(
                out=sc[b].rearrange("p a b -> p (a b)")[0:U],
                in_=sc[b].rearrange("p a b -> p (a b)")[0:U],
                func=AF.Exp, bias=self.consts[0:U, 1:2]))
            r.inc("act", "c_act")
            r.wait("ve", "c_act")
            r.op("ve", lambda e, b=b: e.reduce_sum(
                out=sm[b][0:U], in_=sc[b][0:U], axis=AX.X))
            r.inc("ve", "c_ve")
            r.wait("act", "c_ve")
            r.op("act", lambda e, b=b: e.activation(
                out=sm[b][0:U], in_=sm[b][0:U], func=AF.Ln))
            r.drain("act")
            r.op("act", lambda e, b=b: e.activation(
                out=sm[b][0:U], in_=sm[b][0:U], func=AF.Exp, scale=-1.0))
            rc = r.inc("act", "c_act")
            r.wait("ve", "c_act", rc)
            cv = r.op("ve", lambda e, b=b: e.tensor_tensor(
                out=am[b][0:U], in0=sc[b][0:U],
                in1=sm[b][0:U].unsqueeze(2).broadcast_to([U, 16, 16]),
                op=ALU.mult), inc="c_ve")["c_ve"]
            r.drain("ve")
            a_done[m] = cv
            osum_j = {}
            for jg in range(8):
                jb = jg % 2
                if jg >= 2:
                    r.wait("ve", "c_pe", osum_j[jg - 2])
                eng = "gp" if jg % 2 == 0 else "ve"
                if eng == "ve":
                    cv = ("c_ve", r.op("ve", lambda e, b=b, jg=jg, jb=jb:
                                       e.tensor_tensor(
                        out=ot[jb][0:U],
                        in0=vm[b][0:U, 2 * jg:2 * jg + 2, :].unsqueeze(2)
                        .broadcast_to([U, 2, 16, 64]),
                        in1=am[b][0:U, :, 2 * jg:2 * jg + 2]
                        .transpose([0, 2, 1]).unsqueeze(3)
                        .broadcast_to([U, 2, 16, 64]),
                        op=ALU.mult), inc="c_ve")["c_ve"])
                else:
                    r.wait("gp", "c_ve", a_done[m])
                    if jg >= 2:
                        r.wait("gp", "c_pe", osum_j[jg - 2])
                    cv = ("c_gp", r.op("gp", lambda e, b=b, jg=jg, jb=jb:
                                       e.tensor_tensor(
                        out=ot[jb][0:U],
                        in0=vm[b][0:U, 2 * jg:2 * jg + 2, :].unsqueeze(2)
                        .broadcast_to([U, 2, 16, 64]),
                        in1=am[b][0:U, :, 2 * jg:2 * jg + 2]
                        .transpose([0, 2, 1]).unsqueeze(3)
                        .broadcast_to([U, 2, 16, 64]),
                        op=ALU.mult), inc="c_gp")["c_gp"])
                r.wait("pe", cv[0], cv[1])
                if jg == 0 and m >= 2:
                    r.wait("pe", "c_act", ev_done.get(m - 2, 0))
                for jj in range(2):
                    for ns in range(2):
                        c = r.op("pe", lambda e, b=b, jg=jg, jb=jb, jj=jj,
                                 ns=ns: e.matmul(
                                     self.bank(2 * b + ns)[0:U, :],
                                     self.ident[0:U, 0:U],
                                     ot[jb][:, jj, :, :]
                                     .rearrange("p a b -> p (a b)")
                                     [0:U, ns * 512:(ns + 1) * 512],
                                     start=(jg == 0 and jj == 0),
                                     stop=(jg == 7 and jj == 1)),
                                 inc="c_pe" if ns == 1 else None)
                osum_j[jg] = c["c_pe"]
            osum_done[m] = osum_j[7]
            r.wait("pe", "c_pe", osum_done[m])
            if m >= 1:
                r.wait("pe", "c_act", ev_done.get(m - 1, 0))
            pbf = self.ps2[3][:].bitcast(BF16)
            r.wait("act", "c_pe", osum_done[m])
            ob = r.op("act", lambda e, b=b: e.activation(
                out=t1[b].rearrange("p a b -> p (a b)")[0:U],
                in_=self.pair(2 * b)[0:U, :], func=AF.Copy),
                      inc="c_act")["c_act"]
            r.wait("pe", "c_act", ob)
            for i in range(NH):
                c = r.op("pe", lambda e, b=b, i=i, pbf=pbf: e.matmul(
                    pbf[0:64, i * U:(i + 1) * U],
                    t1[b][0:U, i, :],
                    self.identb[0:U, 0:U],
                    start=True, stop=True, is_transpose=True),
                         inc="c_pe" if i == NH - 1 else None)
            tr_done[m] = c["c_pe"]
            r.wait("act", "c_pe", tr_done[m])
            cc = r.op("act", lambda e, m=m, pbf=pbf: e.activation(
                out=scr[64 * (m % 2):64 * (m % 2) + 64, m // 2, :],
                in_=pbf[0:64, 0:16 * U], func=AF.Copy),
                      inc="c_act")
            ev_done[m] = cc["c_act"]
        r.barrier()
        QS = T // 512
        mm_done = [0] * 8
        evlist = []
        for m in range(8):
            b = m % 2
            if m >= 2:
                r.wait("gp", "c_pe", mm_done[m - 2])
            ddone = r.op("gp", lambda e, m=m, b=b: e.dma_start(
                out=wb[b].rearrange("p a b -> p (a b)"),
                in_=d["wopk"][m * 128:(m + 1) * 128, :]),
                 inc=dsem("gp", b), inck=16)[dsem("gp", b)]
            r.wait("pe", dsem("gp", b), ddone)
            for ns in range(QS):
                g = len(evlist)
                if g >= 4:
                    r.wait("pe", "c_ve", evlist[g - 4])
                for k in range(8):
                    c = r.op("pe", lambda e, m=m, b=b, k=k, ns=ns, g=g:
                             e.matmul(
                        self.bank(g % 4), wb[b][:, k, :],
                        scr[:, k, ns * 512:(ns + 1) * 512],
                        start=(k == 0), stop=(k == 7)),
                             inc="c_pe" if k == 7 else None)
                mm_done[m] = c["c_pe"]
                r.wait("ve", "c_pe", mm_done[m])
                ev = r.op(
                    "ve", lambda e, m=m, b=b, ns=ns, g=g:
                    e.scalar_tensor_tensor(
                        out=self.Xv[:, m, ns * 512:(ns + 1) * 512],
                        in0=self.bank(g % 4),
                        scalar=self.bo_t[:, m:m + 1],
                        in1=self.Xv[:, m, ns * 512:(ns + 1) * 512],
                        op0=ALU.add, op1=ALU.add), inc="c_ve")["c_ve"]
                evlist.append(ev)
                r.drain("ve")
        r.barrier()


    def _store_out(self):
        r = self.r
        r.barrier()
        r.op("sp", lambda e: e.dma_start(
            out=self.dram["out_fm"].rearrange("(c p) t -> p c t", p=128),
            in_=self.Xv), inc="d_spb", inck=16)
        r.wait("sp", "d_spb")


def make_in_maps(inputs):
    shared = {
        "in_proj_w": np.ascontiguousarray(inputs["L0_attn_a_in_proj_w"]),
        "ip_pk": np.ascontiguousarray(
            inputs["L0_attn_a_in_proj_w"].reshape(8, 128, 32, 128)
            .transpose(2, 1, 0, 3).reshape(4096, 1024)),
        "conv_w": np.ascontiguousarray(
            inputs["L0_attn_a_conv_w"].reshape(4, DI)),
        "conv_b": inputs["L0_attn_a_conv_b"],
        "x_proj_w": np.ascontiguousarray(inputs["L0_attn_a_x_proj_w"]),
        "dt_proj_w": np.ascontiguousarray(inputs["L0_attn_a_dt_proj_w"]),
        "dt_proj_b": inputs["L0_attn_a_dt_proj_b"],
        "A_log": np.ascontiguousarray(inputs["L0_attn_a_A_log"]),
        "D_vec": inputs["L0_attn_a_D"],
        "out_proj_w": np.ascontiguousarray(inputs["L0_attn_a_out_proj_w"]),
        "op_pk": np.ascontiguousarray(
            inputs["L0_attn_a_out_proj_w"].reshape(16, 128, 8, 128)
            .transpose(2, 1, 0, 3).reshape(1024, 2048)),
        "wqT": np.ascontiguousarray(inputs["L0_attn_b_Wq"].T),
        "wqpk": np.ascontiguousarray(
            inputs["L0_attn_b_Wq"].T.reshape(8, 128, 1024)
            .transpose(1, 0, 2).reshape(128, 8192)),
        "wkpk": np.ascontiguousarray(
            inputs["L0_attn_b_Wk"].T.reshape(8, 128, 1024)
            .transpose(1, 0, 2).reshape(128, 8192)),
        "wvpk": np.ascontiguousarray(
            inputs["L0_attn_b_Wv"].T.reshape(8, 128, 1024)
            .transpose(1, 0, 2).reshape(128, 8192)),
        "wkT": np.ascontiguousarray(inputs["L0_attn_b_Wk"].T),
        "wvT": np.ascontiguousarray(inputs["L0_attn_b_Wv"].T),
        "woT": np.ascontiguousarray(inputs["L0_attn_b_Wo"].T),
        "wopk": np.ascontiguousarray(
            inputs["L0_attn_b_Wo"].T.reshape(8, 128, 8, 128)
            .transpose(2, 1, 0, 3).reshape(1024, 1024)),
        "bqkv": np.stack([inputs["L0_attn_b_bq"], inputs["L0_attn_b_bk"],
                          inputs["L0_attn_b_bv"]]),
        "bsel": np.stack([np.eye(3, dtype=np.float32)[:, i:i + 1]
                          * np.ones((1, 128), np.float32)
                          for i in range(3)]),
        "bo": inputs["L0_attn_b_bo"],
        "w1": np.ascontiguousarray(inputs["L0_ffn_w1"]),
        "w1T": np.ascontiguousarray(inputs["L0_ffn_w1"].T),
        "w1pk": np.ascontiguousarray(
            inputs["L0_ffn_w1"].T.reshape(8, 128, 32, 128)
            .transpose(2, 1, 0, 3).reshape(4096, 1024)),
        "w2pk": np.ascontiguousarray(
            inputs["L0_ffn_w2"].T.reshape(32, 128, 8, 128)
            .transpose(2, 1, 0, 3).reshape(1024, 4096)),
        "w2": np.ascontiguousarray(inputs["L0_ffn_w2"]),
        "w2T": np.ascontiguousarray(inputs["L0_ffn_w2"].T),
        "b1": inputs["L0_ffn_b1"],
        "b2": inputs["L0_ffn_b2"],
        "ident_in": np.eye(128, dtype=np.float32),
    }
    shared = {k: np.ascontiguousarray(v, dtype=np.float32)
              for k, v in shared.items()}
    in_maps = []
    for c in range(inputs["L0_attn_a_x"].shape[0]):
        m = dict(shared)
        m["x_fm"] = np.ascontiguousarray(inputs["L0_attn_a_x"][c].T)
        in_maps.append(m)
    return in_maps


def kernel(**inputs):
    T = inputs["L0_attn_a_x"].shape[1]
    nc = KB(T=T).build()
    in_maps = make_in_maps(inputs)
    res = run_bass_kernel_spmd(nc, in_maps, core_ids=list(range(NCORES)))
    out = np.stack([np.asarray(res.results[c]["out_fm"]).T
                    for c in range(NCORES)], axis=0)
    return np.ascontiguousarray(out.astype(np.float32))


# version 17
# speedup vs baseline: 1.0069x; 1.0069x over previous
import numpy as np
import concourse.bass as bass
import concourse.mybir as mybir
from concourse.bass_utils import run_bass_kernel_spmd

F32 = mybir.dt.float32
BF16 = mybir.dt.bfloat16
U8 = mybir.dt.uint8
AF = mybir.ActivationFunctionType
ALU = mybir.AluOpType
AX = mybir.AxisListType

D = 1024
DI = 2048
NST = 16
DTR = 128
NH = 16
DH = 64
NP = NH * (DH + 1)
NCORES = 8
ARENA_BYTES = 147296
SBASE = 4416

ENGINES = ("sp", "gp", "pe", "act", "ve")
SEMS = ("d_sp", "d_sp2", "d_spb", "d_gp", "d_gp2", "d_gpb", "c_pe", "c_act",
        "c_ve", "c_gp", "c_sp", "b_sp", "b_gp", "b_pe", "b_act", "b_ve")
BAR = {"sp": "b_sp", "gp": "b_gp", "pe": "b_pe", "act": "b_act", "ve": "b_ve"}


class Rec:

    def __init__(self):
        self.ops = {e: [] for e in ENGINES}
        self.cnt = {s: 0 for s in SEMS}
        self.sems = None

    def _emit(self, eng, fn):
        self.ops[eng].append(fn)

    def op(self, eng, build, inc=None, inck=1):
        if inc is not None:
            self.cnt[inc] += inck
        def f(e, build=build, inc=inc, inck=inck):
            ins = build(e)
            if inc is not None:
                ins.then_inc(self.sems[inc], inck)
        self._emit(eng, f)
        return dict(self.cnt)

    def wait(self, eng, sem, val=None):
        if val is None:
            val = self.cnt[sem]
        if val <= 0:
            return
        self._emit(eng, lambda e, sem=sem, val=val: e.wait_ge(self.sems[sem], val))

    def drain(self, eng):
        self._emit(eng, lambda e: e.drain())

    def inc(self, eng, sem, k=1):
        self.drain(eng)
        self.cnt[sem] += k
        self._emit(eng, lambda e, sem=sem, k=k: e.sem_inc(self.sems[sem], k))
        return self.cnt[sem]

    def qdma(self, queue):
        if queue == "sp":
            for sname in ("d_sp", "d_sp2"):
                self.wait("sp", sname)
        else:
            for sname in ("d_gp", "d_gp2"):
                self.wait("gp", sname)

    def barrier(self):
        for sname in ("d_sp", "d_sp2", "d_spb"):
            self.wait("sp", sname)
        for sname in ("d_gp", "d_gp2", "d_gpb"):
            self.wait("gp", sname)
        snap = {}
        for e in ENGINES:
            snap[e] = self.inc(e, BAR[e])
        for e in ENGINES:
            for o in ENGINES:
                if o != e:
                    self.wait(e, BAR[o], snap[o])



def dsem(queue, b):
    return {"sp": ("d_sp", "d_sp2"), "gp": ("d_gp", "d_gp2")}[queue][b % 2]

def cdiv(a, b):
    return (a + b - 1) // b


class KB:
    def __init__(self, T=2048, stages=("mamba", "prep", "ffn1", "attn", "ffn2")):
        self.T = T
        self.TQ = T // 4
        self.TH = T // 2
        self.stages = stages
        self.nc = bass.Bass("TRN2", target_bir_lowering=False)
        self.r = Rec()
        self.dram = {}
        self._declare_dram()

    def _declare_dram(self):
        nc, T = self.nc, self.T
        def din(name, shape, dt=F32):
            self.dram[name] = nc.dram_tensor(name, shape, dt,
                                             kind="ExternalInput").ap()
        din("x_fm", [D, T])
        din("in_proj_w", [D, 2 * DI])
        din("conv_w", [4, DI])
        din("conv_b", [DI])
        din("x_proj_w", [DI, DTR + 2 * NST])
        din("dt_proj_w", [DTR, DI])
        din("dt_proj_b", [DI])
        din("A_log", [DI, NST])
        din("D_vec", [DI])
        din("out_proj_w", [DI, D])
        din("wqT", [D, D])
        din("wkT", [D, D])
        din("wvT", [D, D])
        din("woT", [D, D])
        din("wopk", [D, D])
        din("wqpk", [128, 8 * D])
        din("wkpk", [128, 8 * D])
        din("wvpk", [128, 8 * D])
        din("bqkv", [3, D])
        din("bsel", [3, 3, 128])
        din("bo", [D])
        din("w1", [4 * D, D])
        din("w1T", [D, 4 * D])
        din("w2", [D, 4 * D])
        din("w2T", [4 * D, D])
        din("w1pk", [4 * D, D])
        din("w2pk", [D, 4 * D])
        din("ip_pk", [2 * DI, D])
        din("op_pk", [D, DI])
        din("b1", [4 * D])
        din("b2", [D])
        din("ident_in", [128, 128])
        self.dram["out_fm"] = nc.dram_tensor("out_fm", [D, T], F32,
                                             kind="ExternalOutput").ap()
        self.dram["bc_hbm"] = nc.dram_tensor("bc_hbm", [2 * NST, self.TQ],
                                             BF16, kind="Internal").ap()
        self.dram["w1s"] = nc.dram_tensor("w1s", [4 * D, D], BF16,
                                          kind="Internal").ap()
        self.dram["w2s"] = nc.dram_tensor("w2s", [D, 4 * D], BF16,
                                          kind="Internal").ap()
        self.dram["r_hbm"] = nc.dram_tensor("r_hbm", [2, 512], F32,
                                            kind="Internal").ap()
        self.dram["ip_bf"] = nc.dram_tensor("ip_bf", [2 * DI, D], BF16,
                                            kind="Internal").ap()
        self.dram["op_bf"] = nc.dram_tensor("op_bf", [D, DI], BF16,
                                            kind="Internal").ap()

    def build(self):
        nc, r = self.nc, self.r
        T = self.T
        from contextlib import ExitStack
        es = ExitStack()
        self.X = es.enter_context(nc.sbuf_tensor("X", [128, 8 * T], F32))
        self.Xv = self.X[:].rearrange("p (c t) -> p c t", c=8)
        self.AR = es.enter_context(
            nc.sbuf_tensor("AR", [128, ARENA_BYTES], U8))
        so = [0]
        def sv(shape, dt=F32):
            esz = 2 if dt == BF16 else 4
            n = int(np.prod(shape[1:])) * esz
            o = so[0]
            so[0] += (n + 63) & ~63
            return self.view(o, dt, shape)
        self.convb_t = sv([128, 16])
        self.cw_t = sv([128, 4, 16])
        self.dtb_t = sv([128, 16])
        self.d_t = sv([128, 16])
        self.a_t = sv([128, 256])
        self.bo_t = sv([128, 8])
        self.b1_t = sv([128, 32])
        self.b2_t = sv([128, 8])
        self.al1_t = sv([128, 32])
        self.al2_t = sv([128, 8])
        self.carry = sv([128, 256])
        self.consts = sv([128, 2])
        self.ident = sv([128, 128], BF16)
        self.identb = self.ident
        self.onesf = sv([128, 128], F32)
        self.onesb = sv([128, 128], BF16)
        self.onesrow = sv([1, 128], BF16)
        self.tails = sv([128, 16, 3], BF16)
        assert so[0] <= SBASE, f"static region {so[0]}"
        self.ps2 = [es.enter_context(nc.psum_tensor(f"ps{i}", [128, 1024], F32))
                    for i in range(4)]
        r.sems = {s: es.enter_context(nc.semaphore(s)) for s in SEMS}

        self._setup_consts()
        self._load_x()
        if "mamba" in self.stages:
            self._mamba()
        if "ffn1" in self.stages:
            self._ffn(first=True)
        if "attn" in self.stages:
            self._attn()
        if "ffn2" in self.stages:
            self._ffn(first=False)
        self._store_out()

        with nc.Block() as block:
            @block.sync
            def _(e):
                for f in r.ops["sp"]: f(e)
            @block.gpsimd
            def _(e):
                for f in r.ops["gp"]: f(e)
            @block.tensor
            def _(e):
                for f in r.ops["pe"]: f(e)
            @block.scalar
            def _(e):
                for f in r.ops["act"]: f(e)
            @block.vector
            def _(e):
                for f in r.ops["ve"]: f(e)
        es.close()
        return nc

    def bank(self, i, w=512):
        return self.ps2[i // 2][:, (i % 2) * 512:(i % 2) * 512 + w]

    def pair(self, i, w=1024):
        assert i % 2 == 0
        return self.ps2[i // 2][:, 0:w]

    def view(self, off, dt, shape):
        esz = 2 if dt == BF16 else 4
        n = int(np.prod(shape[1:]))
        ap = self.AR[0:shape[0], off:off + n * esz].bitcast(dt)
        if len(shape) == 3:
            ap = ap.rearrange("p (a b) -> p a b", a=shape[1])
        return ap

    def _setup_consts(self):
        r, d = self.r, self.dram
        def ld(dst, src):
            r.op("sp", lambda e, dst=dst, src=src: e.dma_start(
                out=dst, in_=src, allow_slow_non_contiguous=True),
                 inc="d_spb", inck=16)
        ld(self.convb_t[:], d["conv_b"].rearrange("(c p) -> p c", p=128))
        ld(self.cw_t[:], d["conv_w"].rearrange("j (c p) -> p j c", p=128))
        ld(self.dtb_t[:], d["dt_proj_b"].rearrange("(c p) -> p c", p=128))
        ld(self.d_t[:], d["D_vec"].rearrange("(c p) -> p c", p=128))
        ld(self.a_t[:].rearrange("p (c n) -> p c n", c=16),
           d["A_log"].rearrange("(c p) n -> p c n", p=128))
        ld(self.bo_t[:], d["bo"].rearrange("(c p) -> p c", p=128))
        ld(self.b1_t[:], d["b1"].rearrange("(c p) -> p c", p=128))
        ld(self.b2_t[:], d["b2"].rearrange("(c p) -> p c", p=128))
        ldc = r.cnt["d_spb"]
        r.wait("sp", "d_spb", ldc)
        r.op("ve", lambda e: e.memset(self.consts[:, 0:1], 1e-5))
        r.op("ve", lambda e: e.memset(self.consts[:, 1:2], -10.0))
        r.op("ve", lambda e: e.memset(self.onesf[:], 1.0 / 1024.0))
        r.op("ve", lambda e: e.memset(self.onesb[:], 1.0 / 1024.0))
        r.op("ve", lambda e: e.memset(self.onesrow[:], 1.0))
        r.op("ve", lambda e: e.memset(self.carry[:], 0.0))
        r.drain("ve")
        r.op("gp", lambda e: e.dma_start(out=self.ident,
                                         in_=d["ident_in"]),
             inc="d_gpb", inck=16)
        r.wait("act", "d_spb", ldc)
        r.op("act", lambda e: e.activation(out=self.a_t[:], in_=self.a_t[:],
                                           func=AF.Exp))
        r.drain("act")
        r.op("act", lambda e: e.activation(out=self.a_t[:], in_=self.a_t[:],
                                           func=AF.Copy, scale=-1.0),
             inc="c_act")
        r.wait("gp", "d_gpb")
        r.barrier()

    def _load_x(self):
        r = self.r
        r.op("sp", lambda e: e.dma_start(
            out=self.Xv,
            in_=self.dram["x_fm"].rearrange("(c p) t -> p c t", p=128)),
             inc="d_spb", inck=16)
        c = self.r.cnt["d_spb"]
        for e in ("ve", "pe", "act", "gp"):
            r.wait(e, "d_spb", c)

    def _layernorm(self, t0, tlen, xn_view, o_ln, bb=0, bb2=2, waits=None):
        r = self.r
        tsl = slice(t0, t0 + tlen)
        xsq = self.view(o_ln, BF16, [128, 8, tlen])
        mean_sb = self.view(o_ln + 8 * tlen * 2, F32, [128, tlen])
        rstd_sb = self.view(o_ln + 8 * tlen * 2 + 4 * tlen, F32, [128, tlen])
        t1 = self.view(o_ln, F32, [128, tlen])
        nsl = cdiv(tlen, 512)
        if waits is None:
            r.wait("act", "c_ve")
        else:
            for eng, sem, val in waits:
                r.wait(eng, sem, val)
        r.op("act", lambda e: e.activation(out=xsq, in_=self.Xv[:, :, tsl],
                                           func=AF.Square), inc="c_act")
        if waits is None:
            r.wait("pe", "c_act")
            r.wait("pe", "c_ve")
        else:
            r.wait("pe", "c_act")
        for ns in range(nsl):
            a, b = ns * 512, min((ns + 1) * 512, tlen)
            for k in range(8):
                r.op("pe", lambda e, k=k, ns=ns, a=a, b=b: e.matmul(
                    self.bank(bb + ns, b - a), self.onesf[:],
                    self.Xv[:, k, t0 + a:t0 + b], start=(k == 0),
                    stop=(k == 7)))
            for k in range(8):
                r.op("pe", lambda e, k=k, ns=ns, a=a, b=b: e.matmul(
                    self.bank(bb2 + ns, b - a), self.onesb[:],
                    xsq[:, k, a:b], start=(k == 0), stop=(k == 7)),
                     inc="c_pe" if (k == 7 and ns == nsl - 1) else None)
        r.wait("act", "c_pe")
        r.op("act", lambda e: e.activation(
            out=mean_sb, in_=self.bank(bb, tlen) if tlen <= 512 else
            self.pair(bb, tlen), func=AF.Copy), inc="c_act")
        r.wait("ve", "c_act")
        r.op("ve", lambda e: e.tensor_tensor(out=rstd_sb, in0=mean_sb,
                                             in1=mean_sb, op=ALU.mult))
        r.drain("ve")
        r.op("ve", lambda e: e.tensor_tensor(
            out=rstd_sb,
            in0=self.bank(bb2, tlen) if tlen <= 512 else self.pair(bb2, tlen),
            in1=rstd_sb, op=ALU.subtract), inc="c_ve")
        r.wait("act", "c_ve")
        r.op("act", lambda e: e.activation(out=rstd_sb, in_=rstd_sb,
                                           func=AF.Ln,
                                           bias=self.consts[:, 0:1]))
        r.drain("act")
        r.op("act", lambda e: e.activation(out=rstd_sb, in_=rstd_sb,
                                           func=AF.Exp, scale=-0.5),
             inc="c_act")
        r.wait("ve", "c_act")
        for c in range(8):
            r.op("ve", lambda e, c=c: e.tensor_tensor(
                out=t1, in0=self.Xv[:, c, tsl], in1=mean_sb,
                op=ALU.subtract))
            r.drain("ve")
            r.op("ve", lambda e, c=c: e.tensor_tensor(
                out=xn_view[:, c, :], in0=t1, in1=rstd_sb, op=ALU.mult),
                 inc="c_ve" if c == 7 else None)
            r.drain("ve")

    def _mamba(self):
        T, TQ, r, d = self.T, self.TQ, self.r, self.dram
        off = [SBASE]
        def alloc(n):
            n = (n + 63) & ~63
            o = off[0]; off[0] += n; return o
        o_xn = alloc(8 * TQ * 2)
        o_xbpre = alloc(16 * (TQ + 3) * 2 + 2)
        o_xbs = alloc(16 * TQ * 2)
        o_zbs = alloc(16 * TQ * 2)
        o_bbc = alloc(16 * TQ * 2)
        o_cbc = alloc(16 * TQ * 2)
        o_dtraw = alloc(TQ * 2)
        o_bc = o_cbc
        o_dt = alloc(2 * TQ * 4)
        o_vdup = alloc(2 * 2 * TQ * 2)
        o_dec = alloc(2 * 2 * TQ * 4)
        o_inp = alloc(2 * 2 * TQ * 2)
        o_h = alloc(2 * 2 * TQ * 2)
        o_hc = alloc(2 * 2 * TQ * 2)
        o_seed = alloc(320)
        o_yv = alloc(2 * TQ * 4)
        o_cacc = o_yv
        o_wi = alloc(2 * 8 * 128 * 2)
        if 16 * TQ * 2 >= 2 * 16 * 128 * 2:
            o_wo = o_cbc
        else:
            o_wo = alloc(2 * 16 * 128 * 2)
        o_xpw = alloc(16 * 160 * 2)
        o_dtw = alloc(16 * 128 * 2)
        o_lnm = o_xbpre
        assert off[0] <= ARENA_BYTES, f"mamba arena {off[0]}"

        xn = self.view(o_xn, BF16, [128, 8, TQ])
        xbpre = self.view(o_xbpre, BF16, [128, 16, TQ + 3])
        xbs = self.view(o_xbs, BF16, [128, 16, TQ])
        zbs = self.view(o_zbs, BF16, [128, 16, TQ])
        bbc = self.view(o_bbc, BF16, [128, 16, TQ])
        cbc = self.view(o_cbc, BF16, [128, 16, TQ])
        dtraw = self.view(o_dtraw, BF16, [128, TQ])
        bcrows = self.view(o_bc, BF16, [128, TQ])
        dt = [self.view(o_dt + i * TQ * 4, F32, [128, TQ]) for i in (0, 1)]
        vdup = [self.view(o_vdup + i * 2 * TQ * 2, BF16, [128, 2, TQ])
                for i in (0, 1)]
        dec_all = [self.view(o_dec + (i % 2) * 2 * TQ * 4, F32, [128, 2, TQ])
                   for i in range(3)]
        inp_all = [self.view(o_inp + i * 2 * TQ * 2, BF16, [128, 2, TQ])
                   for i in (0, 1)]
        h_all = [self.view(o_h + i * 2 * TQ * 2, BF16, [128, 2, TQ])
                 for i in (0, 1)]
        hc_all = [self.view(o_hc + i * 2 * TQ * 2, BF16, [128, 2, TQ])
                  for i in (0, 1)]
        dec0s = [self.view(o_seed + i * 64, F32, [128, 16]) for i in (0, 1)]
        seedv = [self.view(o_seed + 128 + i * 64, F32, [128, 16])
                 for i in (0, 1)]
        yv = [self.view(o_yv + i * TQ * 2, BF16, [128, TQ]) for i in (0, 1)]
        cacc = [self.view(o_cacc + i * TQ * 4, F32, [128, TQ]) for i in (0, 1)]
        wi = [self.view(o_wi + i * 8 * 128 * 2, BF16, [128, 8, 128])
              for i in (0, 1)]
        wo = [self.view(o_wo + i * 16 * 128 * 2, BF16, [128, 16, 128])
              for i in (0, 1)]
        xpw = self.view(o_xpw, BF16, [128, 16, 160])
        dtw = self.view(o_dtw, BF16, [128, 16, 128])
        av = self.a_t[:].rearrange("p (c n) -> p c n", c=16)

        for i in range(2):
            r.op("ve", lambda e, i=i: e.memset(dec_all[i][:, :, 0:1], 0.0))
        r.inc("ve", "c_ve")
        r.op("gp", lambda e: e.dma_start(
            out=xpw, in_=d["x_proj_w"].rearrange("(k p) f -> p k f", p=128)),
             inc="d_gpb", inck=16)
        r.op("gp", lambda e: e.dma_start(
            out=dtw.rearrange("p a b -> p (a b)"), in_=d["dt_proj_w"]),
             inc="d_gpb", inck=16)
        small_w = r.cnt["d_gpb"]
        r.wait("gp", "d_gpb", small_w)
        r.wait("pe", "d_gpb", small_w)

        st = {}

        def do_ln(qq):
            lv = max(st.get("ln_ve", 0), st.get("conv_ve", 0))
            ln_waits = [("act", "c_ve", lv), ("pe", "c_ve", lv)]
            self._layernorm(qq * TQ, TQ, xn, o_lnm, bb=4, bb2=5,
                            waits=ln_waits)
            st["xn_ready"] = r.cnt["c_ve"]
            st["ln_ve"] = r.cnt["c_ve"]
            st["ln_evac"] = r.cnt["c_act"]

        def inproj_block(qq, m, mm_done, ev_done, wr_done, dst):
            b = m % 2
            if qq == 0:
                if m >= 2:
                    r.wait("gp", "c_pe", mm_done[m - 2])
                    r.wait("gp", dsem("sp", b), wr_done[m - 2])
                ddone = r.op("gp", lambda e, m=m, b=b: e.dma_start(
                    out=wi[b].rearrange("p a b -> p (a b)"),
                    in_=d["ip_pk"][m * 128:(m + 1) * 128, :]),
                     inc=dsem("gp", b), inck=16)[dsem("gp", b)]
                r.wait("sp", dsem("gp", b), ddone)
                wr_done[m] = r.op("sp", lambda e, m=m, b=b: e.dma_start(
                    out=d["ip_bf"][m * 128:(m + 1) * 128, :],
                    in_=wi[b].rearrange("p a b -> p (a b)")),
                     inc=dsem("sp", b), inck=16)[dsem("sp", b)]
                st["sp_par"][b] = wr_done[m]
                r.wait("pe", dsem("gp", b), ddone)
            else:
                if m >= 2:
                    r.wait("sp", "c_pe", mm_done[m - 2])
                if m in (0, 1, 16, 17):
                    r.wait("sp", dsem("sp", b), st["sp_par"][b])
                ddone = r.op("sp", lambda e, m=m, b=b: e.dma_start(
                    out=wi[b].rearrange("p a b -> p (a b)"),
                    in_=d["ip_bf"][m * 128:(m + 1) * 128, :]),
                     inc=dsem("sp", b), inck=16)[dsem("sp", b)]
                st["sp_par"][b] = ddone
                r.wait("pe", dsem("sp", b), ddone)
            r.wait("pe", "c_ve", st["xn_ready"])
            if m >= 2:
                r.wait("pe", "c_act", ev_done[m - 2])
            else:
                r.wait("pe", "c_act", st["ln_evac"])
                r.wait("pe", "c_ve", st["ln_var"])
            for k in range(8):
                c = r.op("pe", lambda e, m=m, b=b, k=k: e.matmul(
                    self.bank(4 + b, TQ), wi[b][:, k, :], xn[:, k, :],
                    start=(k == 0), stop=(k == 7)),
                         inc="c_pe" if k == 7 else None)
            mm_done[m] = c["c_pe"]
            r.wait("act", "c_pe", mm_done[m])
            if m < 16:
                c = r.op("act", lambda e, m=m, b=b: e.activation(
                    out=xbpre[:, m, 3:3 + TQ], in_=self.bank(4 + b, TQ),
                    func=AF.Copy), inc="c_act")
            else:
                r.wait("act", "c_pe", st.get("op_mm", 0))
                c = r.op("act", lambda e, m=m, b=b: e.activation(
                    out=zbs[:, m - 16, :], in_=self.bank(4 + b, TQ),
                    func=AF.Silu), inc="c_act")
            ev_done[m] = c["c_act"]
            st["ip_ev"] = c["c_act"]

        def do_conv(qq):
            r.wait("ve", "c_act", st["ip_ev"])
            if qq == 0:
                r.op("ve", lambda e: e.memset(xbpre[:, :, 0:3], 0.0),
                     inc="c_ve")
            else:
                r.op("ve", lambda e: e.tensor_copy(out=xbpre[:, :, 0:3],
                                                   in_=self.tails),
                     inc="c_ve")
            r.drain("ve")
            silu_done = [0] * 16
            for c in range(16):
                b = c % 2
                acc = cacc[b]
                if c >= 2:
                    r.wait("ve", "c_act", silu_done[c - 2])
                r.op("ve", lambda e, c=c, acc=acc: e.tensor_scalar(
                    out=acc, in0=xbpre[:, c, 0:TQ],
                    scalar1=self.cw_t[:, 0, c:c + 1], scalar2=None,
                    op0=ALU.mult))
                r.drain("ve")
                for j in range(1, 4):
                    r.op("ve", lambda e, c=c, j=j, acc=acc:
                         e.scalar_tensor_tensor(
                             out=acc, in0=xbpre[:, c, j:j + TQ],
                             scalar=self.cw_t[:, j, c:c + 1], in1=acc,
                             op0=ALU.mult, op1=ALU.add))
                    r.drain("ve")
                cv = r.inc("ve", "c_ve")
                r.wait("act", "c_ve", cv)
                cc = r.op("act", lambda e, c=c, acc=acc: e.activation(
                    out=xbs[:, c, :], in_=acc, func=AF.Silu,
                    bias=self.convb_t[:, c:c + 1]), inc="c_act")
                silu_done[c] = cc["c_act"]
            r.wait("ve", "c_act", silu_done[15])
            r.op("ve", lambda e: e.tensor_copy(out=self.tails,
                                               in_=xbpre[:, :, TQ:TQ + 3]),
                 inc="c_ve")
            st["xbs_ready"] = silu_done[15]
            st["conv_ve"] = r.cnt["c_ve"]

        def do_xproj(qq):
            r.wait("pe", "c_act", st["xbs_ready"])
            for k in range(16):
                r.op("pe", lambda e, k=k: e.matmul(
                    self.bank(4, TQ), xpw[:, k, 0:128], xbs[:, k, :],
                    start=(k == 0), stop=(k == 15)))
            for k in range(16):
                r.op("pe", lambda e, k=k: e.matmul(
                    self.bank(5, TQ)[0:32, :], xpw[:, k, 128:160],
                    xbs[:, k, :], start=(k == 0), stop=(k == 15)),
                     inc="c_pe" if k == 15 else None)
            r.wait("act", "c_pe")
            r.op("act", lambda e: e.activation(out=dtraw,
                                               in_=self.bank(4, TQ),
                                               func=AF.Copy))
            r.op("act", lambda e: e.activation(out=bcrows[0:32, :],
                                               in_=self.bank(5, TQ)[0:32, :],
                                               func=AF.Copy), inc="c_act")
            st["dtraw_ready"] = r.cnt["c_act"]
            r.wait("sp", "c_act", st["dtraw_ready"])
            r.wait("sp", "c_ve")
            r.wait("sp", "d_spb")
            for sname in ("d_sp", "d_sp2", "d_gp", "d_gp2"):
                r.wait("sp", sname)
            r.op("sp", lambda e: e.dma_start(out=d["bc_hbm"][:, 0:TQ],
                                             in_=bcrows[0:32, :]),
                 inc="d_spb", inck=16)
            r.wait("sp", "d_spb")
            bcall = self.view(o_bbc, BF16, [128, 2 * NST, TQ])
            r.op("sp", lambda e, bcall=bcall: e.dma_start(
                out=bcall,
                in_=d["bc_hbm"][:, 0:TQ].unsqueeze(0)
                .broadcast_to([128, 2 * NST, TQ])),
                 inc="d_spb", inck=16)
            st["bc_done"] = r.cnt["d_spb"]

        def do_outproj(qq, o_state):
            tsl = slice(qq * TQ, (qq + 1) * TQ)
            r.qdma("gp")
            r.wait("sp", "d_spb")
            r.wait("sp", "c_ve", o_state["ep"])
            r.wait("sp", "c_gp")
            r.wait("gp", "c_ve", o_state["ep"])
            r.wait("pe", "c_ve", o_state["ep"])
            o_mm = [0] * 8
            o_ev = [0] * 8
            o_wr = [0] * 8
            for m in range(8):
                b = m % 2
                if m >= 2:
                    r.wait("gp" if qq == 0 else "sp", "c_pe", o_mm[m - 2])
                if qq == 0:
                    if m >= 2:
                        r.wait("gp", dsem("sp", b), o_wr[m - 2])
                    ddone = r.op("gp", lambda e, m=m, b=b: e.dma_start(
                        out=wo[b].rearrange("p a b -> p (a b)"),
                        in_=d["op_pk"][m * 128:(m + 1) * 128, :]),
                         inc=dsem("gp", b), inck=16)[dsem("gp", b)]
                    r.wait("sp", dsem("gp", b), ddone)
                    if m < 2:
                        r.wait("sp", dsem("sp", b), st["sp_par"][b])
                    o_wr[m] = r.op("sp", lambda e, m=m, b=b: e.dma_start(
                        out=d["op_bf"][m * 128:(m + 1) * 128, :],
                        in_=wo[b].rearrange("p a b -> p (a b)")),
                         inc=dsem("sp", b), inck=16)[dsem("sp", b)]
                    st["sp_par"][b] = o_wr[m]
                    r.wait("pe", dsem("gp", b), ddone)
                else:
                    if m < 2:
                        r.wait("sp", dsem("sp", b), st["sp_par"][b])
                    ddone = r.op("sp", lambda e, m=m, b=b: e.dma_start(
                        out=wo[b].rearrange("p a b -> p (a b)"),
                        in_=d["op_bf"][m * 128:(m + 1) * 128, :]),
                         inc=dsem("sp", b), inck=16)[dsem("sp", b)]
                    st["sp_par"][b] = ddone
                    r.wait("pe", dsem("sp", b), ddone)
                if m >= 2:
                    r.wait("pe", "c_ve", o_ev[m - 2])
                for k in range(16):
                    c = r.op("pe", lambda e, m=m, b=b, k=k: e.matmul(
                        self.bank(2 + b, TQ), wo[b][:, k, :], zbs[:, k, :],
                        start=(k == 0), stop=(k == 15)),
                             inc="c_pe" if k == 15 else None)
                o_mm[m] = c["c_pe"]
                r.wait("ve", "c_pe", o_mm[m])
                o_ev[m] = r.op("ve", lambda e, m=m, b=b, tsl=tsl:
                               e.tensor_tensor(
                                   out=self.Xv[:, m, tsl],
                                   in0=self.bank(2 + b, TQ),
                                   in1=self.Xv[:, m, tsl], op=ALU.add),
                               inc="c_ve")["c_ve"]
            st["op_mm"] = o_mm[7]

        def do_scan(qq, pieces):
            dt_ev = [0, 0]
            vv = [0, 0]
            sv_ = [0, 0]
            dec_done = {}
            inp_done = {}
            carry_done = {}
            h_done = {}
            hc_done = {}
            ymm_done = {}
            ep_done = [0]
            G = 2
            NG = NST // G
            bc_done = st["bc_done"]
            dtraw_ready = st["dtraw_ready"]
            xbs_ready = st["xbs_ready"]
            zb_ready = st["ip_ev"]

            def prologue(dd):
                b = dd % 2
                if dd >= 2:
                    r.wait("pe", "c_act", dt_ev[b])
                else:
                    r.wait("pe", "c_act", dtraw_ready)
                cpe = r.op("pe", lambda e, dd=dd, b=b: e.matmul(
                    self.bank(6 + b, TQ), dtw[:, dd, :], dtraw,
                    start=True, stop=True), inc="c_pe")["c_pe"]
                r.wait("act", "c_pe", cpe)
                if dd >= 2:
                    r.wait("act", "c_ve", vv[b])
                    r.wait("act", "c_act", dec_done[(dd - 2, NG - 1)])
                r.op("act", lambda e, dd=dd, b=b: e.activation(
                    out=dt[b], in_=self.bank(6 + b, TQ), func=AF.Exp,
                    bias=self.dtb_t[:, dd:dd + 1]))
                r.drain("act")
                dt_ev[b] = r.op("act", lambda e, b=b: e.activation(
                    out=dt[b], in_=dt[b], func=AF.Ln, bias=1.0),
                                inc="c_act")["c_act"]
                r.drain("act")
                r.wait("ve", "c_act", dt_ev[b])
                r.wait("ve", "c_act", xbs_ready)
                r.wait("ve", "d_spb", bc_done)
                r.op("ve", lambda e, dd=dd, b=b: e.tensor_tensor(
                    out=vdup[b][:, 0, :], in0=dt[b], in1=xbs[:, dd, :],
                    op=ALU.mult))
                r.drain("ve")
                r.op("ve", lambda e, b=b: e.tensor_copy(
                    out=vdup[b][:, 1, :], in_=vdup[b][:, 0, :]))
                r.op("ve", lambda e, b=b, dd=dd: e.tensor_tensor(
                    out=dec0s[b], in0=av[:, dd, :],
                    in1=dt[b][:, 0:1].broadcast_to([128, 16]), op=ALU.mult),
                     inc="c_ve")
                r.drain("ve")
                vv[b] = r.cnt["c_ve"]
                r.wait("act", "c_ve", vv[b])
                r.op("act", lambda e, b=b: e.activation(
                    out=dec0s[b], in_=dec0s[b], func=AF.Exp), inc="c_act")
                d0 = r.cnt["c_act"]
                r.wait("ve", "c_act", d0)
                r.op("ve", lambda e, dd=dd, b=b: e.tensor_tensor(
                    out=seedv[b], in0=dec0s[b],
                    in1=self.carry[:, dd * 16:dd * 16 + 16], op=ALU.mult))
                r.drain("ve")
                r.op("ve", lambda e, b=b: e.tensor_tensor(
                    out=dec0s[b], in0=bbc[:, 0:16, 0],
                    in1=vdup[b][:, 0, 0:1].broadcast_to([128, 16]),
                    op=ALU.mult))
                r.drain("ve")
                sv_[b] = r.op("ve", lambda e, b=b: e.tensor_tensor(
                    out=seedv[b], in0=seedv[b], in1=dec0s[b], op=ALU.add),
                              inc="c_ve")["c_ve"]
                r.drain("ve")

            def groups(dd):
                b = dd % 2
                for ng in range(NG):
                    gb = (dd * NG + ng) % 2
                    hb = ng % 2
                    ib = ng % 2
                    key = (dd, ng)
                    keyp2 = (dd, ng - 2) if ng >= 2 else (dd - 1, NG - 2 + ng)
                    nbase = ng * G
                    idx = dd * 16 + nbase
                    if dd + ng > 0:
                        r.wait("act", "c_ve", h_done.get(keyp2, 0))
                    for n in range(G):
                        r.op("act", lambda e, dd=dd, b=b, gb=gb, n=n,
                             nbase=nbase: e.activation(
                                 out=dec_all[gb][:, n, 1:TQ],
                                 in_=dt[b][:, 1:TQ], func=AF.Exp,
                                 scale=av[:, dd, nbase + n:nbase + n + 1]))
                    r.wait("act", "c_ve", sv_[b])
                    ca = r.op("act", lambda e, b=b, ib=ib, nbase=nbase:
                              e.activation(
                                  out=inp_all[ib][:, :, 0],
                                  in_=seedv[b][:, nbase:nbase + G],
                                  func=AF.Copy), inc="c_act")
                    col0_done = ca["c_act"]
                    dec_done[key] = col0_done
                    r.wait("ve", "d_spb", bc_done)
                    r.op("ve", lambda e, b=b, ib=ib, nbase=nbase:
                         e.tensor_tensor(
                             out=inp_all[ib][:, :, 1:],
                             in0=vdup[b][:, :, 1:],
                             in1=bbc[:, nbase:nbase + G, 1:],
                             op=ALU.mult))
                    r.drain("ve")
                    r.wait("ve", "c_act", col0_done)
                    r.op("ve", lambda e, gb=gb, hb=hb, ib=ib:
                         e.tensor_tensor_scan(
                             out=h_all[hb].rearrange("p a b -> p (a b)"),
                             data0=dec_all[gb].rearrange("p a b -> p (a b)"),
                             data1=inp_all[ib].rearrange("p a b -> p (a b)"),
                             initial=0.0, op0=ALU.mult, op1=ALU.add))
                    hd = r.inc("ve", "c_ve")
                    h_done[key] = hd
                    r.wait("act", "c_ve", hd)
                    carry_done[key] = r.op(
                        "act", lambda e, hb=hb, idx=idx: e.activation(
                            out=self.carry[:, idx:idx + G],
                            in_=h_all[hb][:, :, TQ - 1], func=AF.Copy),
                        inc="c_act")["c_act"]
                    if dd + ng > 0:
                        r.wait("ve", "c_pe", ymm_done.get(keyp2, 0))
                    hc = r.op("ve", lambda e, hb=hb, nbase=nbase:
                              e.tensor_tensor(
                                  out=hc_all[hb], in0=h_all[hb],
                                  in1=cbc[:, nbase:nbase + G, :],
                                  op=ALU.mult), inc="c_ve")["c_ve"]
                    hc_done[key] = ("c_ve", hc)
                    r.wait("pe", "c_ve", hc)
                    for n in range(G):
                        c = r.op("pe", lambda e, b=b, hb=hb, n=n, ng=ng:
                                 e.matmul(
                                     self.bank(b, TQ), self.ident[:],
                                     hc_all[hb][:, n, :],
                                     start=(ng == 0 and n == 0),
                                     stop=(ng == NG - 1 and n == G - 1)),
                                 inc="c_pe" if n == G - 1 else None)
                    ymm_done[key] = c["c_pe"]
                r.wait("ve", "c_pe", ymm_done[(dd, NG - 1)])
                r.wait("ve", "c_act", zb_ready)
                r.op("ve", lambda e, dd=dd, b=b: e.scalar_tensor_tensor(
                    out=yv[b], in0=xbs[:, dd, :],
                    scalar=self.d_t[:, dd:dd + 1], in1=self.bank(b, TQ),
                    op0=ALU.mult, op1=ALU.add))
                r.drain("ve")
                ep_done[0] = r.op("ve", lambda e, dd=dd, b=b: e.tensor_tensor(
                    out=zbs[:, dd, :], in0=yv[b], in1=zbs[:, dd, :],
                    op=ALU.mult), inc="c_ve")["c_ve"]

            for dd in range(17):
                if dd < 16:
                    prologue(dd)
                if dd >= 1:
                    groups(dd - 1)
                for _ in range(2):
                    if pieces:
                        pieces.pop(0)()
            while pieces:
                pieces.pop(0)()
            return ep_done[0]

        st["ln_ve"] = 0
        st["ln_var"] = 0
        st["sp_par"] = [0, 0]
        do_ln(0)
        st["ln_var"] = r.cnt["c_ve"]
        mm0, ev0, wr0 = [0] * 32, [0] * 32, [0] * 32
        for m in range(32):
            inproj_block(0, m, mm0, ev0, wr0, None)
        do_conv(0)
        do_xproj(0)
        for q in range(4):
            pieces = []
            if q < 3:
                qq = q + 1
                mmw, evw, wrw = [0] * 32, [0] * 32, [0] * 32
                def mk_ln(qq=qq):
                    def f():
                        do_ln(qq)
                        st["ln_var"] = r.cnt["c_ve"]
                    return f
                pieces.append(mk_ln())
                for m in range(16):
                    def mk_ip(qq=qq, m=m, mmw=mmw, evw=evw, wrw=wrw):
                        return lambda: inproj_block(qq, m, mmw, evw, wrw, None)
                    pieces.append(mk_ip())
            ep = do_scan(q, pieces)
            do_outproj(q, {"ep": ep})
            if q < 3:
                qq = q + 1
                for m in range(16, 32):
                    inproj_block(qq, m, mmw, evw, wrw, None)
                do_conv(qq)
                do_xproj(qq)
        r.barrier()


    def _ffn_prep(self):
        r, d = self.r, self.dram
        o_st = SBASE
        red = self.view(o_st + 2 * 4096 * 4, F32, [128, 1])
        red_done = [0, 0]
        for wname, aout, nb, nw, inv in (
                ("w1", self.al1_t, 32, 1024, 1.0 / 1024.0),
                ("w2", self.al2_t, 8, 4096, 1.0 / 4096.0)):
            stage = [self.view(o_st + i * nw * 4, F32, [128, nw])
                     for i in (0, 1)]
            red_done = [0, 0]
            for m in range(nb):
                b = m % 2
                if m >= 2:
                    r.wait("sp", "c_ve", red_done[b])
                else:
                    r.wait("sp", "c_ve")
                ddone = r.op("sp", lambda e, m=m, b=b, wname=wname,
                             stage=stage:
                     e.dma_start(out=stage[b],
                                 in_=d[wname][m * 128:(m + 1) * 128, :]),
                     inc=dsem("sp", b), inck=16)[dsem("sp", b)]
                r.wait("ve", dsem("sp", b), ddone)
                r.op("ve", lambda e, b=b, stage=stage: e.reduce_sum(
                    out=red, in_=stage[b], axis=AX.X,
                    apply_absolute_value=True))
                r.drain("ve")
                red_done[b] = r.op(
                    "ve", lambda e, m=m, aout=aout, inv=inv: e.tensor_scalar(
                        out=aout[:, m:m + 1], in0=red, scalar1=inv,
                        scalar2=None, op0=ALU.mult), inc="c_ve")["c_ve"]
                r.drain("ve")
        o_sg = o_st + 2 * 4096 * 4 + 64
        sgi = [self.view(o_sg + i * 32 * 128 * 4, F32, [128, 32, 128])
               for i in (0, 1)]
        sgo = [self.view(o_sg + 2 * 32 * 128 * 4 + i * 32 * 128 * 2, BF16,
                         [128, 32, 128]) for i in (0, 1)]
        r.wait("sp", "c_ve")
        st_done = [0, 0]
        sg_done = [0, 0]
        for wname, sname, nk, nb in (("w1T", "w1s", 8, 32),
                                     ("w2T", "w2s", 32, 8)):
            for m in range(nb):
                b = m % 2
                if st_done[b]:
                    r.wait("sp", dsem("sp", b), st_done[b])
                    r.wait("sp", "c_act", sg_done[b])
                ddone = r.op("sp", lambda e, m=m, b=b, wname=wname, nk=nk:
                     e.dma_start(out=sgi[b][:, 0:nk, :],
                                 in_=d[wname][:, m * 128:(m + 1) * 128]
                                 .rearrange("(k p) f -> p k f", p=128)),
                     inc=dsem("sp", b), inck=16)[dsem("sp", b)]
                r.wait("act", dsem("sp", b), ddone)
                if st_done[b]:
                    r.wait("act", dsem("sp", b), st_done[b])
                r.op("act", lambda e, b=b, nk=nk: e.sign(
                    out=sgo[b][:, 0:nk, :], in_=sgi[b][:, 0:nk, :]))
                sg_done[b] = r.inc("act", "c_act")
                r.wait("sp", "c_act", sg_done[b])
                st_done[b] = r.op(
                    "sp", lambda e, m=m, b=b, sname=sname, nk=nk:
                    e.dma_start(out=d[sname][:, m * 128:(m + 1) * 128]
                                .rearrange("(k p) f -> p k f", p=128),
                                in_=sgo[b][:, 0:nk, :]),
                    inc=dsem("sp", b), inck=16)[dsem("sp", b)]
        r.barrier()

    def _ffn(self, first=False):
        T, TH, r, d = self.T, self.TH, self.r, self.dram
        off = [SBASE]
        def alloc(n):
            n = (n + 63) & ~63
            o = off[0]; off[0] += n; return o
        o_xn = alloc(8 * TH * 2)
        o_h = alloc(32 * TH * 2)
        o_w1 = alloc(4 * 8 * 128 * 2)
        o_w2 = alloc(2 * 16 * 128 * 2)
        o_g1 = alloc(4 * 8 * 128 * 4)
        o_g2 = alloc(2 * 16 * 128 * 4)
        o_tmp = alloc(2 * TH * 4)
        o_ln = o_h
        assert off[0] <= ARENA_BYTES, f"ffn arena {off[0]}"
        xn = self.view(o_xn, BF16, [128, 8, TH])
        hact = self.view(o_h, BF16, [128, 32, TH])
        w1b = [self.view(o_w1 + i * 8 * 128 * 2, BF16, [128, 8, 128])
               for i in range(4)]
        w2h = [self.view(o_w2 + i * 16 * 128 * 2, BF16, [128, 16, 128])
               for i in (0, 1)]
        sg1 = [self.view(o_g1 + i * 8 * 128 * 4, F32, [128, 8, 128])
               for i in range(4)]
        sg2 = [self.view(o_g2 + i * 16 * 128 * 4, F32, [128, 16, 128])
               for i in (0, 1)]
        tmp = [self.view(o_tmp + i * TH * 4, F32, [128, TH]) for i in (0, 1)]
        nsl = cdiv(TH, 512)

        for hh in range(2):
            prep = first and hh == 0
            t0 = hh * TH
            tsl = slice(t0, t0 + TH)
            self._layernorm(t0, TH, xn, o_ln)
            r.wait("pe", "c_ve")
            r.qdma("sp")
            mm_done = [0] * 32
            ev_done = [0] * 32
            st_done = [0] * 4
            ab_done = [0] * 4
            r.qdma("gp")
            for m in range(32):
                b = m % 2
                B = m % 4
                q = "sp" if m % 2 == 0 else "gp"
                pb = (m // 2) % 2
                sem = dsem(q, pb)
                if m >= 4:
                    r.wait(q, "c_pe", mm_done[m - 4])
                if prep:
                    if st_done[B]:
                        r.wait(q, sem, st_done[B])
                    ddone = r.op(q, lambda e, m=m, B=B: e.dma_start(
                        out=sg1[B].rearrange("p a b -> p (a b)"),
                        in_=d["w1pk"][m * 128:(m + 1) * 128, :]),
                         inc=sem, inck=16)[sem]
                    r.wait("act", sem, ddone)
                    r.op("act", lambda e, B=B: e.sign(out=w1b[B],
                                                      in_=sg1[B]))
                    r.drain("act")
                    r.op("act", lambda e, B=B: e.activation(
                        out=sg1[B], in_=sg1[B], func=AF.Abs))
                    ab_done[B] = r.inc("act", "c_act")
                    r.wait("pe", "c_act", ab_done[B])
                    for k in range(8):
                        r.op("pe", lambda e, m=m, B=B, k=k: e.matmul(
                            self.bank(0, 512)[:, m:m + 1], sg1[B][:, k, :],
                            self.onesf[:, 0:1], start=(k == 0),
                            stop=(k == 7)))
                    r.wait(q, "c_act", ab_done[B])
                    st_done[B] = r.op(
                        q, lambda e, m=m, B=B: e.dma_start(
                            out=d["w1s"][m * 128:(m + 1) * 128, :],
                            in_=w1b[B].rearrange("p a b -> p (a b)")),
                        inc=sem, inck=16)[sem]
                else:
                    ddone = r.op(q, lambda e, m=m, B=B: e.dma_start(
                        out=w1b[B].rearrange("p a b -> p (a b)"),
                        in_=d["w1s"][m * 128:(m + 1) * 128, :]),
                         inc=sem, inck=16)[sem]
                    r.wait("pe", sem, ddone)
                if m >= 2:
                    r.wait("pe", "c_act", ev_done[m - 2])
                for ns in range(nsl):
                    a, bnd = ns * 512, min((ns + 1) * 512, TH)
                    for k in range(8):
                        c = r.op("pe", lambda e, m=m, b=b, B=B, k=k, a=a,
                                 bnd=bnd: e.matmul(
                                     self.pair(4 + 2 * b, TH)[:, a:bnd],
                                     w1b[B][:, k, :], xn[:, k, a:bnd],
                                     start=(k == 0), stop=(k == 7)),
                                 inc="c_pe" if k == 7 and bnd == TH else None)
                mm_done[m] = c["c_pe"]
                r.wait("act", "c_pe", mm_done[m])
                if prep:
                    r.op("act", lambda e, m=m: e.activation(
                        out=self.al1_t[:, m:m + 1],
                        in_=self.bank(0, 512)[:, m:m + 1], func=AF.Copy))
                    r.drain("act")
                ev_done[m] = r.op("act", lambda e, m=m, b=b: e.activation(
                    out=hact[:, m, :], in_=self.pair(4 + 2 * b, TH),
                    func=AF.Gelu, scale=self.al1_t[:, m:m + 1],
                    bias=self.b1_t[:, m:m + 1]), inc="c_act")["c_act"]
            r.qdma("sp")
            r.wait("pe", "c_act")
            m2_done = [0] * 16
            e2_done = [0] * 8
            al2ev = [0] * 8
            st2 = [0, 0]
            for m in range(8):
                bm = m % 2
                for h in range(2):
                    sb = 2 * m + h
                    b2 = sb % 2
                    k0 = 16 * h
                    if sb >= 2:
                        r.wait("sp", "c_pe", m2_done[sb - 2])
                    if prep:
                        if st2[b2]:
                            r.wait("sp", dsem("sp", b2), st2[b2])
                        ddone = r.op("sp", lambda e, m=m, b2=b2, k0=k0:
                                     e.dma_start(
                                         out=sg2[b2]
                                         .rearrange("p a b -> p (a b)"),
                                         in_=d["w2pk"][
                                             m * 128:(m + 1) * 128,
                                             k0 * 128:(k0 + 16) * 128]),
                             inc=dsem("sp", b2), inck=16)[dsem("sp", b2)]
                        r.wait("act", dsem("sp", b2), ddone)
                        if st2[b2]:
                            r.wait("act", dsem("sp", b2), st2[b2])
                        if sb >= 2:
                            r.wait("act", "c_pe", m2_done[sb - 2])
                        r.op("act", lambda e, b2=b2: e.sign(out=w2h[b2],
                                                            in_=sg2[b2]))
                        r.drain("act")
                        r.op("act", lambda e, b2=b2: e.activation(
                            out=sg2[b2], in_=sg2[b2], func=AF.Abs))
                        ab2 = r.inc("act", "c_act")
                        r.wait("pe", "c_act", ab2)
                        for k in range(16):
                            r.op("pe", lambda e, m=m, b2=b2, k=k, h=h:
                                 e.matmul(
                                     self.bank(4, 512)[:, m:m + 1],
                                     sg2[b2][:, k, :], self.onesf[:, 0:1],
                                     start=(h == 0 and k == 0),
                                     stop=(h == 1 and k == 15)))
                        r.wait("sp", "c_act", ab2)
                        st2[b2] = r.op(
                            "sp", lambda e, m=m, b2=b2, k0=k0: e.dma_start(
                                out=d["w2s"][m * 128:(m + 1) * 128,
                                             k0 * 128:(k0 + 16) * 128],
                                in_=w2h[b2].rearrange("p a b -> p (a b)")),
                            inc=dsem("sp", b2), inck=16)[dsem("sp", b2)]
                    else:
                        ddone = r.op("sp", lambda e, m=m, b2=b2, k0=k0:
                                     e.dma_start(
                                         out=w2h[b2]
                                         .rearrange("p a b -> p (a b)"),
                                         in_=d["w2s"][
                                             m * 128:(m + 1) * 128,
                                             k0 * 128:(k0 + 16) * 128]),
                             inc=dsem("sp", b2), inck=16)[dsem("sp", b2)]
                        r.wait("pe", dsem("sp", b2), ddone)
                    if h == 0 and m >= 2:
                        r.wait("pe", "c_ve", e2_done[m - 2])
                    for ns in range(nsl):
                        a, bnd = ns * 512, min((ns + 1) * 512, TH)
                        for k in range(16):
                            c = r.op("pe", lambda e, m=m, bm=bm, b2=b2, k=k,
                                     k0=k0, h=h, a=a, bnd=bnd: e.matmul(
                                         self.pair(0 + 2 * bm, TH)[:, a:bnd],
                                         w2h[b2][:, k, :],
                                         hact[:, k0 + k, a:bnd],
                                         start=(h == 0 and k == 0),
                                         stop=(h == 1 and k == 15)),
                                     inc="c_pe" if k == 15 and bnd == TH
                                     else None)
                    m2_done[sb] = c["c_pe"]
                r.wait("ve", "c_pe", m2_done[2 * m + 1])
                if prep:
                    r.wait("act", "c_pe", m2_done[2 * m + 1])
                    al2ev[m] = r.op("act", lambda e, m=m: e.activation(
                        out=self.al2_t[:, m:m + 1],
                        in_=self.bank(4, 512)[:, m:m + 1], func=AF.Copy,
                        scale=0.25), inc="c_act")["c_act"]
                    r.wait("ve", "c_act", al2ev[m])
                r.op("ve", lambda e, m=m, bm=bm: e.tensor_scalar(
                    out=tmp[bm], in0=self.pair(0 + 2 * bm, TH),
                    scalar1=self.al2_t[:, m:m + 1],
                    scalar2=self.b2_t[:, m:m + 1],
                    op0=ALU.mult, op1=ALU.add))
                r.drain("ve")
                e2_done[m] = r.op(
                    "ve", lambda e, m=m, bm=bm, tsl=tsl: e.tensor_tensor(
                        out=self.Xv[:, m, tsl], in0=tmp[bm],
                        in1=self.Xv[:, m, tsl], op=ALU.add),
                    inc="c_ve")["c_ve"]
            r.barrier()


    def _attn(self):
        T, r, d = self.T, self.r, self.dram
        U = T // 16
        assert U <= 128
        off = [SBASE]
        def alloc(n):
            n = (n + 63) & ~63
            o = off[0]; off[0] += n; return o
        o_xn = alloc(8 * T * 2)
        o_scr = alloc(8 * T * 2)
        o_wq = alloc(8 * 1024 * 2)
        o_wk = alloc(8 * 1024 * 2)
        o_wv = alloc(8 * 1024 * 2)
        o_qm = alloc(2 * 1024 * 2)
        o_km = alloc(2 * 1024 * 2)
        o_vm = alloc(2 * 1024 * 2)
        o_s = alloc(2 * 256 * 4)
        o_a = alloc(2 * 256 * 2)
        o_sm = alloc(2 * 64)
        o_ot = alloc(2 * 2048 * 2)
        o_brow = alloc(2 * 8 * 128 * 2)
        o_bsel = o_brow + 2048
        o_wb = o_brow
        lnsz = 8 * min(1024, T) * 2 + 8 * min(1024, T)
        o_ln = o_scr if lnsz <= 8 * T * 2 else alloc(lnsz)
        assert off[0] <= ARENA_BYTES, f"attn arena {off[0]}"
        xn = self.view(o_xn, BF16, [128, 8, T])
        scr = self.view(o_scr, BF16, [128, 8, T])
        wq = self.view(o_wq, BF16, [128, 8, 1024])
        wk = self.view(o_wk, BF16, [128, 8, 1024])
        wv = self.view(o_wv, BF16, [128, 8, 1024])
        qm = [self.view(o_qm + i * 2048, BF16, [128, 16, 64]) for i in (0, 1)]
        km = [self.view(o_km + i * 2048, BF16, [128, 16, 64]) for i in (0, 1)]
        vm = [self.view(o_vm + i * 2048, BF16, [128, 16, 64]) for i in (0, 1)]
        sc = [self.view(o_s + i * 1024, F32, [128, 16, 16]) for i in (0, 1)]
        am = [self.view(o_a + i * 512, BF16, [128, 16, 16]) for i in (0, 1)]
        sm = [self.view(o_sm + i * 64, F32, [128, 16]) for i in (0, 1)]
        t1 = qm
        ot = [self.view(o_ot + i * 4096, BF16, [128, 2 * 16, 64])
              .rearrange("p (a b) c -> p a b c", a=2) for i in (0, 1)]
        brow3 = self.view(o_brow, BF16, [3, 1024])
        bsel = self.view(o_bsel, BF16, [3, 3, 128])
        wb = [self.view(o_wb + i * 8 * 128 * 2, BF16, [128, 8, 128])
              for i in (0, 1)]

        for hh in range(cdiv(T, 1024)):
            tl = min(1024, T - hh * 1024)
            self._layernorm(hh * 1024, tl,
                            xn[:, :, hh * 1024:hh * 1024 + tl], o_ln)
        r.qdma("gp")
        for i, wname in enumerate(("wqpk", "wkpk", "wvpk")):
            r.op("gp", lambda e, i=i, wname=wname, wq=wq, wk=wk, wv=wv:
                 e.dma_start(out=(wq, wk, wv)[i]
                             .rearrange("p a b -> p (a b)"),
                             in_=d[wname]),
                 inc="d_gpb", inck=16)
        r.op("gp", lambda e: e.dma_start(out=brow3, in_=d["bqkv"]),
             inc="d_gpb", inck=16)
        r.op("gp", lambda e: e.dma_start(out=bsel, in_=d["bsel"]),
             inc="d_gpb", inck=16)
        wdone = r.cnt["d_gpb"]
        r.wait("gp", "d_gpb", wdone)
        r.wait("pe", "d_gpb", wdone)
        r.wait("pe", "c_ve")
        proj_ev = {}
        sc_red = {}
        a_done = {}
        t1_done = {}
        osum_done = {}
        tr_done = {}
        ev_done = {}
        gpred_done = {}
        for m in range(16):
            b = m % 2
            for wi_, (wt, dst) in enumerate(((wq, qm), (wk, km), (wv, vm))):
                prev = proj_ev.get("last")
                if prev:
                    r.wait("pe", "c_act", prev)
                if wi_ == 0 and m >= 2:
                    r.wait("act", "c_pe", tr_done.get(m - 2, 0))
                    r.wait("pe", "c_pe", tr_done.get(m - 2, 0))
                for ns in range(2):
                    for k in range(8):
                        r.op("pe", lambda e, m=m, k=k, ns=ns, wt=wt:
                             e.matmul(
                                 self.pair(4)[0:U, ns * 512:(ns + 1) * 512],
                                 xn[:, k, m::16],
                                 wt[:, k, ns * 512:(ns + 1) * 512],
                                 start=(k == 0), stop=False))
                    c = r.op("pe", lambda e, wi_=wi_, ns=ns: e.matmul(
                        self.pair(4)[0:U, ns * 512:(ns + 1) * 512],
                        bsel[:, wi_, 0:U],
                        brow3[:, ns * 512:(ns + 1) * 512],
                        start=False, stop=True),
                             inc="c_pe" if ns == 1 else None)
                r.wait("act", "c_pe", c["c_pe"])
                cc = r.op("act", lambda e, b=b, wi_=wi_, dst=dst:
                          e.activation(
                              out=dst[b].rearrange("p a b -> p (a b)")[0:U],
                              in_=self.pair(4)[0:U, :],
                              func=AF.Copy,
                              scale=0.125 if wi_ == 0 else 1.0),
                          inc="c_act")
                proj_ev[(m, wi_)] = cc["c_act"]
                proj_ev["last"] = cc["c_act"]
            r.wait("ve", "c_act", proj_ev[(m, 1)])
            if m >= 2:
                r.wait("ve", "c_ve", a_done.get(m - 2, 0))
            for jg in range(8):
                jb2 = jg % 2
                r.op("ve", lambda e, b=b, jg=jg, jb2=jb2: e.tensor_tensor(
                    out=ot[jb2][0:U],
                    in0=qm[b][0:U].unsqueeze(1).broadcast_to([U, 2, 16, 64]),
                    in1=km[b][0:U, 2 * jg:2 * jg + 2, :].unsqueeze(2)
                    .broadcast_to([U, 2, 16, 64]),
                    op=ALU.mult))
                r.drain("ve")
                cv = r.op("ve", lambda e, b=b, jg=jg, jb2=jb2: e.reduce_sum(
                    out=sc[b][0:U, :, 2 * jg:2 * jg + 2].transpose([0, 2, 1]),
                    in_=ot[jb2][0:U], axis=AX.X), inc="c_ve")["c_ve"]
            sc_red[m] = cv
            r.wait("act", "c_ve", sc_red[m])
            r.op("act", lambda e, b=b: e.activation(
                out=sc[b].rearrange("p a b -> p (a b)")[0:U],
                in_=sc[b].rearrange("p a b -> p (a b)")[0:U],
                func=AF.Exp, bias=self.consts[0:U, 1:2]))
            r.inc("act", "c_act")
            r.wait("ve", "c_act")
            r.op("ve", lambda e, b=b: e.reduce_sum(
                out=sm[b][0:U], in_=sc[b][0:U], axis=AX.X))
            r.inc("ve", "c_ve")
            r.wait("act", "c_ve")
            r.op("act", lambda e, b=b: e.activation(
                out=sm[b][0:U], in_=sm[b][0:U], func=AF.Ln))
            r.drain("act")
            r.op("act", lambda e, b=b: e.activation(
                out=sm[b][0:U], in_=sm[b][0:U], func=AF.Exp, scale=-1.0))
            rc = r.inc("act", "c_act")
            r.wait("ve", "c_act", rc)
            cv = r.op("ve", lambda e, b=b: e.tensor_tensor(
                out=am[b][0:U], in0=sc[b][0:U],
                in1=sm[b][0:U].unsqueeze(2).broadcast_to([U, 16, 16]),
                op=ALU.mult), inc="c_ve")["c_ve"]
            r.drain("ve")
            a_done[m] = cv
            osum_j = {}
            for jg in range(8):
                jb = jg % 2
                if jg >= 2:
                    r.wait("ve", "c_pe", osum_j[jg - 2])
                eng = "gp" if jg % 2 == 0 else "ve"
                if eng == "ve":
                    cv = ("c_ve", r.op("ve", lambda e, b=b, jg=jg, jb=jb:
                                       e.tensor_tensor(
                        out=ot[jb][0:U],
                        in0=vm[b][0:U, 2 * jg:2 * jg + 2, :].unsqueeze(2)
                        .broadcast_to([U, 2, 16, 64]),
                        in1=am[b][0:U, :, 2 * jg:2 * jg + 2]
                        .transpose([0, 2, 1]).unsqueeze(3)
                        .broadcast_to([U, 2, 16, 64]),
                        op=ALU.mult), inc="c_ve")["c_ve"])
                else:
                    r.wait("gp", "c_ve", a_done[m])
                    if jg >= 2:
                        r.wait("gp", "c_pe", osum_j[jg - 2])
                    cv = ("c_gp", r.op("gp", lambda e, b=b, jg=jg, jb=jb:
                                       e.tensor_tensor(
                        out=ot[jb][0:U],
                        in0=vm[b][0:U, 2 * jg:2 * jg + 2, :].unsqueeze(2)
                        .broadcast_to([U, 2, 16, 64]),
                        in1=am[b][0:U, :, 2 * jg:2 * jg + 2]
                        .transpose([0, 2, 1]).unsqueeze(3)
                        .broadcast_to([U, 2, 16, 64]),
                        op=ALU.mult), inc="c_gp")["c_gp"])
                r.wait("pe", cv[0], cv[1])
                if jg == 0 and m >= 2:
                    r.wait("pe", "c_act", ev_done.get(m - 2, 0))
                for jj in range(2):
                    for ns in range(2):
                        c = r.op("pe", lambda e, b=b, jg=jg, jb=jb, jj=jj,
                                 ns=ns: e.matmul(
                                     self.bank(2 * b + ns)[0:U, :],
                                     self.ident[0:U, 0:U],
                                     ot[jb][:, jj, :, :]
                                     .rearrange("p a b -> p (a b)")
                                     [0:U, ns * 512:(ns + 1) * 512],
                                     start=(jg == 0 and jj == 0),
                                     stop=(jg == 7 and jj == 1)),
                                 inc="c_pe" if ns == 1 else None)
                osum_j[jg] = c["c_pe"]
            osum_done[m] = osum_j[7]
            r.wait("pe", "c_pe", osum_done[m])
            if m >= 1:
                r.wait("pe", "c_act", ev_done.get(m - 1, 0))
            pbf = self.ps2[3][:].bitcast(BF16)
            r.wait("act", "c_pe", osum_done[m])
            ob = r.op("act", lambda e, b=b: e.activation(
                out=t1[b].rearrange("p a b -> p (a b)")[0:U],
                in_=self.pair(2 * b)[0:U, :], func=AF.Copy),
                      inc="c_act")["c_act"]
            r.wait("pe", "c_act", ob)
            for i in range(NH):
                c = r.op("pe", lambda e, b=b, i=i, pbf=pbf: e.matmul(
                    pbf[0:64, i * U:(i + 1) * U],
                    t1[b][0:U, i, :],
                    self.identb[0:U, 0:U],
                    start=True, stop=True, is_transpose=True),
                         inc="c_pe" if i == NH - 1 else None)
            tr_done[m] = c["c_pe"]
            r.wait("act", "c_pe", tr_done[m])
            cc = r.op("act", lambda e, m=m, pbf=pbf: e.activation(
                out=scr[64 * (m % 2):64 * (m % 2) + 64, m // 2, :],
                in_=pbf[0:64, 0:16 * U], func=AF.Copy),
                      inc="c_act")
            ev_done[m] = cc["c_act"]
        r.barrier()
        QS = T // 512
        mm_done = [0] * 8
        evlist = []
        for m in range(8):
            b = m % 2
            if m >= 2:
                r.wait("gp", "c_pe", mm_done[m - 2])
            ddone = r.op("gp", lambda e, m=m, b=b: e.dma_start(
                out=wb[b].rearrange("p a b -> p (a b)"),
                in_=d["wopk"][m * 128:(m + 1) * 128, :]),
                 inc=dsem("gp", b), inck=16)[dsem("gp", b)]
            r.wait("pe", dsem("gp", b), ddone)
            for ns in range(QS):
                g = len(evlist)
                if g >= 4:
                    r.wait("pe", "c_ve", evlist[g - 4])
                for k in range(8):
                    c = r.op("pe", lambda e, m=m, b=b, k=k, ns=ns, g=g:
                             e.matmul(
                        self.bank(g % 4), wb[b][:, k, :],
                        scr[:, k, ns * 512:(ns + 1) * 512],
                        start=(k == 0), stop=(k == 7)),
                             inc="c_pe" if k == 7 else None)
                mm_done[m] = c["c_pe"]
                r.wait("ve", "c_pe", mm_done[m])
                ev = r.op(
                    "ve", lambda e, m=m, b=b, ns=ns, g=g:
                    e.scalar_tensor_tensor(
                        out=self.Xv[:, m, ns * 512:(ns + 1) * 512],
                        in0=self.bank(g % 4),
                        scalar=self.bo_t[:, m:m + 1],
                        in1=self.Xv[:, m, ns * 512:(ns + 1) * 512],
                        op0=ALU.add, op1=ALU.add), inc="c_ve")["c_ve"]
                evlist.append(ev)
                r.drain("ve")
        r.barrier()


    def _store_out(self):
        r = self.r
        r.barrier()
        r.op("sp", lambda e: e.dma_start(
            out=self.dram["out_fm"].rearrange("(c p) t -> p c t", p=128),
            in_=self.Xv), inc="d_spb", inck=16)
        r.wait("sp", "d_spb")


def make_in_maps(inputs):
    shared = {
        "in_proj_w": np.ascontiguousarray(inputs["L0_attn_a_in_proj_w"]),
        "ip_pk": np.ascontiguousarray(
            inputs["L0_attn_a_in_proj_w"].reshape(8, 128, 32, 128)
            .transpose(2, 1, 0, 3).reshape(4096, 1024)),
        "conv_w": np.ascontiguousarray(
            inputs["L0_attn_a_conv_w"].reshape(4, DI)),
        "conv_b": inputs["L0_attn_a_conv_b"],
        "x_proj_w": np.ascontiguousarray(inputs["L0_attn_a_x_proj_w"]),
        "dt_proj_w": np.ascontiguousarray(inputs["L0_attn_a_dt_proj_w"]),
        "dt_proj_b": inputs["L0_attn_a_dt_proj_b"],
        "A_log": np.ascontiguousarray(inputs["L0_attn_a_A_log"]),
        "D_vec": inputs["L0_attn_a_D"],
        "out_proj_w": np.ascontiguousarray(inputs["L0_attn_a_out_proj_w"]),
        "op_pk": np.ascontiguousarray(
            inputs["L0_attn_a_out_proj_w"].reshape(16, 128, 8, 128)
            .transpose(2, 1, 0, 3).reshape(1024, 2048)),
        "wqT": np.ascontiguousarray(inputs["L0_attn_b_Wq"].T),
        "wqpk": np.ascontiguousarray(
            inputs["L0_attn_b_Wq"].T.reshape(8, 128, 1024)
            .transpose(1, 0, 2).reshape(128, 8192)),
        "wkpk": np.ascontiguousarray(
            inputs["L0_attn_b_Wk"].T.reshape(8, 128, 1024)
            .transpose(1, 0, 2).reshape(128, 8192)),
        "wvpk": np.ascontiguousarray(
            inputs["L0_attn_b_Wv"].T.reshape(8, 128, 1024)
            .transpose(1, 0, 2).reshape(128, 8192)),
        "wkT": np.ascontiguousarray(inputs["L0_attn_b_Wk"].T),
        "wvT": np.ascontiguousarray(inputs["L0_attn_b_Wv"].T),
        "woT": np.ascontiguousarray(inputs["L0_attn_b_Wo"].T),
        "wopk": np.ascontiguousarray(
            inputs["L0_attn_b_Wo"].T.reshape(8, 128, 8, 128)
            .transpose(2, 1, 0, 3).reshape(1024, 1024)),
        "bqkv": np.stack([inputs["L0_attn_b_bq"], inputs["L0_attn_b_bk"],
                          inputs["L0_attn_b_bv"]]),
        "bsel": np.stack([np.eye(3, dtype=np.float32)[:, i:i + 1]
                          * np.ones((1, 128), np.float32)
                          for i in range(3)]),
        "bo": inputs["L0_attn_b_bo"],
        "w1": np.ascontiguousarray(inputs["L0_ffn_w1"]),
        "w1T": np.ascontiguousarray(inputs["L0_ffn_w1"].T),
        "w1pk": np.ascontiguousarray(
            inputs["L0_ffn_w1"].T.reshape(8, 128, 32, 128)
            .transpose(2, 1, 0, 3).reshape(4096, 1024)),
        "w2pk": np.ascontiguousarray(
            inputs["L0_ffn_w2"].T.reshape(32, 128, 8, 128)
            .transpose(2, 1, 0, 3).reshape(1024, 4096)),
        "w2": np.ascontiguousarray(inputs["L0_ffn_w2"]),
        "w2T": np.ascontiguousarray(inputs["L0_ffn_w2"].T),
        "b1": inputs["L0_ffn_b1"],
        "b2": inputs["L0_ffn_b2"],
        "ident_in": np.eye(128, dtype=np.float32),
    }
    shared = {k: np.ascontiguousarray(v, dtype=np.float32)
              for k, v in shared.items()}
    in_maps = []
    for c in range(inputs["L0_attn_a_x"].shape[0]):
        m = dict(shared)
        m["x_fm"] = np.ascontiguousarray(inputs["L0_attn_a_x"][c].T)
        in_maps.append(m)
    return in_maps


def kernel(**inputs):
    T = inputs["L0_attn_a_x"].shape[1]
    nc = KB(T=T).build()
    in_maps = make_in_maps(inputs)
    res = run_bass_kernel_spmd(nc, in_maps, core_ids=list(range(NCORES)))
    out = np.stack([np.asarray(res.results[c]["out_fm"]).T
                    for c in range(NCORES)], axis=0)
    return np.ascontiguousarray(out.astype(np.float32))
